# version 11
# baseline (speedup 1.0000x reference)
import math
import os
from contextlib import ExitStack
S1_MODE = os.environ.get("K_S1_MODE", "full")
S1_NT = int(os.environ.get("K_S1_NT", "999"))
S1_QRH = int(os.environ.get("K_S1_QRH", "16"))

import numpy as np

import concourse.bass as bass
import concourse.mybir as mybir
from concourse.bass_utils import run_bass_kernel_spmd
from concourse.ap import AP

F32 = mybir.dt.float32
BF16 = mybir.dt.bfloat16
AF = mybir.ActivationFunctionType
ALU = mybir.AluOpType
AX = mybir.AxisListType

H = 16
D = 64
HID = 1024
DH = 4096
CHUNK = 32
PAST, FUT = 31, 8
CTX = 71
SEQ = 2048
NSEQ = 4
NTOK = NSEQ * SEQ
TT = 512
NT = NTOK // TT
NGS = SEQ // 128
NG = NSEQ * NGS
KPAD = SEQ + 64
GROW = 72 * 128
HSTR = 1 + NTOK * 72 + 7

QR_ORDER = [0, 2, 4, 6, 8, 10, 12, 14, 1, 3, 5, 7, 9, 11, 13, 15]
QR_SLOT = {h: i for i, h in enumerate(QR_ORDER)}

Q_SCALE = D ** (-0.5) / math.log(2)
K_SCALE = math.log(1.0 + math.e) / math.log(2)
SOFTCAP = 50.0

STRICT_SAME_ENGINE = True

ENGINES = ("tensor", "vector", "scalar", "gpsimd", "sync")
DMA_QUEUES = ("gpsimd", "sync", "scalar")
NDMA_SLOTS = 4


class Sched:

    def __init__(self, nc, ctx, block):
        self.nc = nc
        self.block = block
        names = list(ENGINES)
        for q in DMA_QUEUES:
            for s in range(NDMA_SLOTS):
                names.append(f"dq_{q}{s}")
        self.sems = {n: ctx.enter_context(nc.semaphore(f"s_{n}")) for n in names}
        self.count = {n: 0 for n in names}
        self.queues = {e: [] for e in ENGINES}
        self.dma_uses = {q: 0 for q in DMA_QUEUES}
        self.last_wait = {e: {n: 0 for n in names} for e in ENGINES}
        self.pending_barrier = {e: {} for e in ENGINES}

    def emit(self, eng, fn_name, *args, deps=(), inc=True, dma=False, **kwargs):
        if dma:
            slot = self.dma_uses[eng] % NDMA_SLOTS
            self.dma_uses[eng] += 1
            sem_name = f"dq_{eng}{slot}"
        else:
            sem_name = eng
        waits = dict(self.pending_barrier[eng])
        self.pending_barrier[eng] = {}
        if dma and self.count[sem_name] > 0:
            if self.count[sem_name] > waits.get(sem_name, 0):
                waits[sem_name] = self.count[sem_name]
        for tok in deps:
            if tok is None:
                continue
            n, v = tok
            if n == eng and not STRICT_SAME_ENGINE:
                continue
            if v > waits.get(n, 0):
                waits[n] = v
        real_waits = []
        for n, v in waits.items():
            if v > self.last_wait[eng][n]:
                real_waits.append((n, v))
                self.last_wait[eng][n] = v
        tok = None
        if inc:
            self.count[sem_name] += 16 if dma else 1
            tok = (sem_name, self.count[sem_name])

        def step(engobj, fn_name=fn_name, args=args, kwargs=kwargs,
                 real_waits=real_waits, inc=inc, dma=dma, sem_name=sem_name):
            for n, v in real_waits:
                engobj.wait_ge(self.sems[n], v)
            inst = getattr(engobj, fn_name)(*args, **kwargs)
            if inc:
                inst.then_inc(self.sems[sem_name], 16 if dma else 1)

        self.queues[eng].append(step)
        return tok

    def raw(self, eng, fn):
        self.queues[eng].append(fn)

    def barrier(self):
        snap = {n: v for n, v in self.count.items() if v > 0}
        for e in ENGINES:
            merged = dict(self.pending_barrier[e])
            for n, v in snap.items():
                if v > merged.get(n, 0):
                    merged[n] = v
            self.pending_barrier[e] = merged

    def flush(self):
        block = self.block
        for e in ENGINES:
            q = self.queues[e]
            if not q:
                continue
            self.queues[e] = []
            section = getattr(block, e)

            def body(engobj, q=q):
                for step in q:
                    step(engobj)

            section(body)

    def final_wait(self):
        snap = {n: v for n, v in self.count.items() if v > 0}

        def body(engobj):
            for n, v in snap.items():
                engobj.wait_ge(self.sems[n], v)

        self.block.sync(body)


def build_program(upto=99, dump=()):
    nc = bass.Bass("TRN2", target_bir_lowering=False, debug=False)

    def dram(name, shape, dtype, kind="Internal"):
        k = "ExternalOutput" if name in dump else kind
        return nc.dram_tensor(name, shape, dtype, kind=k).ap()

    x_fm = nc.dram_tensor("x_fm", [NSEQ, HID, SEQ], F32, kind="ExternalInput").ap()
    wqT = nc.dram_tensor("wqT", [HID, HID], F32, kind="ExternalInput").ap()
    wkT = nc.dram_tensor("wkT", [HID, HID], F32, kind="ExternalInput").ap()
    wvT = nc.dram_tensor("wvT", [HID, HID], F32, kind="ExternalInput").ap()
    wrelT = nc.dram_tensor("wrelT", [HID, HID], F32, kind="ExternalInput").ap()
    wpostT = nc.dram_tensor("wpostT", [HID, HID], F32, kind="ExternalInput").ap()
    posT = nc.dram_tensor("posT", [HID, CTX], F32, kind="ExternalInput").ap()
    pdsd = nc.dram_tensor("pds", [D], F32, kind="ExternalInput").ap()
    w1T = nc.dram_tensor("w1T", [HID, DH], F32, kind="ExternalInput").ap()
    w2T = nc.dram_tensor("w2T", [HID, DH], F32, kind="ExternalInput").ap()
    h1T = nc.dram_tensor("h1T", [DH, DH], F32, kind="ExternalInput").ap()
    h2T = nc.dram_tensor("h2T", [DH, DH], F32, kind="ExternalInput").ap()
    w3T = nc.dram_tensor("w3T", [DH, HID], F32, kind="ExternalInput").ap()
    s1d = nc.dram_tensor("s1", [128, 32], F32, kind="ExternalInput").ap()
    s2d = nc.dram_tensor("s2", [128, 32], F32, kind="ExternalInput").ap()
    h1sd = nc.dram_tensor("h1s", [128, 32], F32, kind="ExternalInput").ap()
    h2sd = nc.dram_tensor("h2s", [128, 32], F32, kind="ExternalInput").ap()
    w3sd = nc.dram_tensor("w3s", [128, 8], F32, kind="ExternalInput").ap()

    out_fm = nc.dram_tensor("out_fm", [NSEQ, HID, SEQ], F32,
                            kind="ExternalOutput").ap()

    q_dram = dram("q_dram", [128, 8, NTOK], BF16)
    k_dram = dram("k_dram", [128, 8, NSEQ * KPAD], BF16)
    v_dram = dram("v_dram", [NSEQ * KPAD, HID], BF16)
    g_dram = dram("g_dram", [H, HSTR], F32)
    attn_dram = dram("attn_dram", [128, 8, NTOK], BF16)
    x1f_dram = dram("x1f_dram", [128, 8, NTOK], F32)
    a_dram = dram("a_dram", [128, 32, NTOK], BF16)
    b_dram = dram("b_dram", [128, 32, NTOK], BF16)
    c_dram = dram("c_dram", [128, 32, NTOK], BF16)

    ctx0 = ExitStack()
    with ctx0:
        block = ctx0.enter_context(nc.Block())
        sch = Sched(nc, ctx0, block)

        ident = ctx0.enter_context(nc.sbuf_tensor("ident", [128, 128], BF16))
        sq128 = ctx0.enter_context(nc.sbuf_tensor("sq128", [128, 8], F32))
        relk = ctx0.enter_context(nc.sbuf_tensor("relk", [128, 8, 72], BF16))
        ones_bf = ctx0.enter_context(nc.sbuf_tensor("ones_bf", [128, 2], BF16))
        ones128 = ctx0.enter_context(nc.sbuf_tensor("ones128", [128, 128], BF16))
        ones_f1 = ctx0.enter_context(nc.sbuf_tensor("ones_f1", [1, 128], F32))
        pb = [ctx0.enter_context(nc.psum_tensor(f"pb{i}", [128, 512], F32))
              for i in range(8)]
        pb_free = {i: None for i in range(8)}

        s1v = ctx0.enter_context(nc.sbuf_tensor("s1v", [128, 32], F32))
        s2v = ctx0.enter_context(nc.sbuf_tensor("s2v", [128, 32], F32))
        h1v = ctx0.enter_context(nc.sbuf_tensor("h1v", [128, 32], F32))
        h2v = ctx0.enter_context(nc.sbuf_tensor("h2v", [128, 32], F32))
        w3v = ctx0.enter_context(nc.sbuf_tensor("w3v", [128, 8], F32))
        zeros = ctx0.enter_context(nc.sbuf_tensor("zeros", [128, 1024], BF16))
        zerof = ctx0.enter_context(nc.sbuf_tensor("zerof", [1, 16], F32))
        epsv = ctx0.enter_context(nc.sbuf_tensor("epsv", [128, 1], F32))

        E = sch.emit

        for i in range(8):
            pb_free[i] = E("vector", "memset", pb[i][:], 0.0)
        t_id0 = E("gpsimd", "memset", ident[:], 0.0)
        t_id = E("gpsimd", "affine_select", out=ident[:], in_=ident[:],
                 compare_op=ALU.not_equal, fill=1.0, base=0,
                 pattern=[[-1, 128]], channel_multiplier=1, deps=[t_id0])
        t_ones = E("gpsimd", "memset", ones_bf[:], 1.0)
        t_o128 = E("gpsimd", "memset", ones128[:], 1.0)
        t_ones1 = E("gpsimd", "memset", ones_f1[:], 1.0)
        t_zero = E("gpsimd", "memset", zeros[:], 0.0)
        t_zerof = E("gpsimd", "memset", zerof[:], 0.0)
        t_eps = E("gpsimd", "memset", epsv[:], 1e-5)

        t_s1 = E("sync", "dma_start", s1v[:], s1d, dma=True)
        t_s2 = E("sync", "dma_start", s2v[:], s2d, dma=True)
        t_h1 = E("sync", "dma_start", h1v[:], h1sd, dma=True)
        t_h2 = E("sync", "dma_start", h2v[:], h2sd, dma=True)
        t_w3 = E("sync", "dma_start", w3v[:], w3sd, dma=True)

        pds_ap = pdsd.rearrange("(p o) -> p o", o=1)
        t_p0 = E("sync", "dma_start", sq128[0:64, 1:2], pds_ap, dma=True)
        t_p1 = E("sync", "dma_start", sq128[64:128, 1:2], pds_ap, dma=True)
        t_a = E("scalar", "activation", sq128[:, 2:3], sq128[:, 1:2], AF.Abs,
                deps=[t_p0, t_p1])
        t_e = E("scalar", "activation", sq128[:, 3:4], sq128[:, 2:3], AF.Exp,
                scale=-1.0, deps=[t_a])
        t_l = E("scalar", "activation", sq128[:, 4:5], sq128[:, 3:4], AF.Ln,
                bias=1.0, deps=[t_e])
        t_m = E("vector", "tensor_scalar_max", sq128[:, 5:6], sq128[:, 1:2],
                0.0, deps=[t_p0, t_p1])
        t_sp = E("vector", "tensor_add", sq128[:, 6:7], sq128[:, 5:6],
                 sq128[:, 4:5], deps=[t_m, t_l])
        t_sq = E("vector", "tensor_scalar_mul", sq128[:, 0:1], sq128[:, 6:7],
                 Q_SCALE, deps=[t_sp])

        pad_toks = []
        for b in range(NSEQ):
            for off in (0, KPAD - 32):
                pad_toks.append(E("sync", "dma_start",
                                  k_dram[:, :, b * KPAD + off:b * KPAD + off + 32],
                                  zeros[:, 0:256].rearrange("p (a w) -> p a w", a=8),
                                  dma=True, deps=[t_zero]))
                pad_toks.append(E("sync", "dma_start",
                                  v_dram[b * KPAD + off:b * KPAD + off + 32, :],
                                  zeros[0:32, :], dma=True, deps=[t_zero]))
        t_guard = E("sync", "dma_start", g_dram[:, 0:1], zerof[0:1, :],
                    dma=True, deps=[t_zerof], allow_slow_non_contiguous=True)

        if upto >= 1:
            with ExitStack() as sctx:
                wq_bf = sctx.enter_context(nc.sbuf_tensor("wq_bf", [128, 8, HID], BF16))
                wk_bf = sctx.enter_context(nc.sbuf_tensor("wk_bf", [128, 8, HID], BF16))
                wv_bf = sctx.enter_context(nc.sbuf_tensor("wv_bf", [128, 8, HID], BF16))
                wr_bf = sctx.enter_context(nc.sbuf_tensor("wr_bf", [128, 8, HID], BF16))
                posb = sctx.enter_context(nc.sbuf_tensor("posb", [128, 8, 72], BF16))
                xbf = [sctx.enter_context(nc.sbuf_tensor(f"xbf{i}", [128, 8, TT], BF16))
                       for i in range(2)]
                qst = [sctx.enter_context(nc.sbuf_tensor(f"qst{i}", [128, 8, TT], BF16))
                       for i in range(2)]
                kst = [sctx.enter_context(nc.sbuf_tensor(f"kst{i}", [128, 8, TT], BF16))
                       for i in range(2)]
                vst = [sctx.enter_context(nc.sbuf_tensor(f"vst{i}", [128, 4, HID], BF16))
                       for i in range(2)]
                gst = [sctx.enter_context(nc.sbuf_tensor(f"gst{i}", [128, 4, H, 72], F32))
                       for i in range(2)]

                def wload(dst, src):
                    return E("gpsimd", "dma_start", dst[:],
                             src.rearrange("(ic p) o -> p ic o", p=128), dma=True)

                t_wq = wload(wq_bf, wqT)
                t_wk = wload(wk_bf, wkT)
                t_wv = wload(wv_bf, wvT)
                t_wr = wload(wr_bf, wrelT)
                t_posz = E("gpsimd", "memset", posb[:], 0.0)
                t_pos = E("gpsimd", "dma_start", posb[:, :, 0:71],
                          posT.rearrange("(ic p) l -> p ic l", p=128),
                          dma=True, deps=[t_posz])

                t_relk = []
                for oc in range(8 if S1_MODE != "w" else 0):
                    bk = 6 + (oc & 1)
                    mt = None
                    for ic in range(8):
                        mt = E("tensor", "matmul", pb[bk][:, 0:72],
                               wr_bf[:, ic, 128 * oc:128 * oc + 128],
                               posb[:, ic, :],
                               start=(ic == 0), stop=(ic == 7),
                               deps=[t_wr, t_pos, pb_free[bk]] if ic == 0 else [],
                               inc=(ic == 7))
                    ev = E("vector", "tensor_copy", relk[:, oc, :], pb[bk][:, 0:72],
                           deps=[mt])
                    pb_free[bk] = ev
                    t_relk.append(ev)

                qkv_done = [[None] * 3 for _ in range(NT)]
                g_done = [None] * NT
                st_free = [None, None]
                for it in range(min(NT, S1_NT) if S1_MODE not in ("w", "relk") else 0):
                    par = it & 1
                    b, t0 = it // 4, TT * (it % 4)
                    xb, qs, ks, vs, gs = xbf[par], qst[par], kst[par], vst[par], gst[par]
                    free_dep = st_free[par] or []
                    t_x = E("gpsimd", "dma_start", xb[:],
                            x_fm[b].rearrange("(ic p) t -> p ic t", p=128)
                            [:, :, t0:t0 + TT],
                            dma=True, deps=[t_wq] + list(free_dep))
                    ev_q = []
                    ev_k = []
                    for proj, wbf, stg, evl in (
                            ("q", wq_bf, qs, ev_q), ("k", wk_bf, ks, ev_k))[
                                :0 if S1_MODE == "x" else 2]:
                        for oc in range(8):
                            bk = oc & 1
                            mt = None
                            for ic in range(8):
                                mt = E("tensor", "matmul", pb[bk][:, 0:TT],
                                       wbf[:, ic, 128 * oc:128 * oc + 128],
                                       xb[:, ic, :],
                                       start=(ic == 0), stop=(ic == 7),
                                       deps=[t_x, pb_free[bk]] if ic == 0 else [],
                                       inc=(ic == 7))
                            if proj == "q":
                                ev = E("scalar", "activation", stg[:, oc, :],
                                       pb[bk][:, 0:TT], AF.Copy,
                                       scale=sq128[:, 0:1], deps=[mt, t_sq])
                            else:
                                ev = E("scalar", "activation", stg[:, oc, :],
                                       pb[bk][:, 0:TT], AF.Copy,
                                       scale=K_SCALE, deps=[mt])
                            pb_free[bk] = ev
                            evl.append(ev)
                    ev_v = []
                    for sub in range(4 if S1_MODE in ("full", "v", "qr") else 0):
                        for nh in range(2):
                            bk = 2 + nh
                            mt = None
                            for ic in range(8):
                                mt = E("tensor", "matmul", pb[bk][:, 0:512],
                                       xb[:, ic, 128 * sub:128 * sub + 128],
                                       wv_bf[:, ic, 512 * nh:512 * nh + 512],
                                       start=(ic == 0), stop=(ic == 7),
                                       deps=[t_x, t_wv, pb_free[bk]] if ic == 0 else [],
                                       inc=(ic == 7))
                            ev = E("scalar", "activation",
                                   vs[:, sub, 512 * nh:512 * nh + 512],
                                   pb[bk][:, 0:512], AF.Copy, deps=[mt])
                            pb_free[bk] = ev
                            ev_v.append(ev)
                    ev_g = []
                    for sub in range(4 if S1_MODE in ("full", "qr", "qr_nostore", "qr_noevac") else 0):
                        n_qr = min(16, S1_QRH)
                        for q4 in range(4):
                            heads = [h for h in QR_ORDER[4 * q4:4 * q4 + 4]
                                     if h < n_qr or S1_QRH >= 16]
                            if not heads:
                                continue
                            bk = 4 + q4
                            mts = []
                            for j, h in enumerate(heads):
                                mts.append(E(
                                    "tensor", "matmul",
                                    pb[bk][:, 128 * j:128 * j + 72],
                                    qs[64 * (h & 1):64 * (h & 1) + 64,
                                       h >> 1, 128 * sub:128 * sub + 128],
                                    relk[64 * (h & 1):64 * (h & 1) + 64, h >> 1, :],
                                    start=True, stop=True,
                                    deps=([ev_q[h >> 1], t_relk[h >> 1],
                                           pb_free[bk]] if j == 0 else
                                          [ev_q[h >> 1], t_relk[h >> 1]]),
                                    inc=(j == len(heads) - 1)))
                            if S1_MODE == "qr_noevac":
                                pb_free[bk] = mts[-1]
                                continue
                            ev = E("scalar", "activation",
                                   gs[:, sub, 4 * q4:4 * q4 + len(heads), :],
                                   pb[bk][:, 0:512].rearrange(
                                       "p (a w) -> p a w", a=4)[:, 0:len(heads), 0:72],
                                   AF.Copy,
                                   scale=1.0 / SOFTCAP, deps=[mts[-1]])
                            pb_free[bk] = ev
                            ev_g.append((sub, ev))
                    all_st = []
                    if ev_q:
                        tq = E("sync", "dma_start",
                               q_dram[:, :, 2048 * b + t0:2048 * b + t0 + TT],
                               qs[:], dma=True, deps=ev_q)
                        tk = E("sync", "dma_start",
                               k_dram[:, :, KPAD * b + 32 + t0:KPAD * b + 32 + t0 + TT],
                               ks[:], dma=True, deps=ev_k)
                        all_st += [tq, tk]
                    tvs = []
                    for sub in range(4 if ev_v else 0):
                        r0 = KPAD * b + 32 + t0 + 128 * sub
                        tvs.append(E("sync", "dma_start", v_dram[r0:r0 + 128, :],
                                     vs[:, sub, :], dma=True, deps=[ev_v[2 * sub],
                                                                    ev_v[2 * sub + 1]]))
                    all_st += tvs
                    tgs = []
                    for sub in range(4 if (ev_g and S1_MODE != "qr_nostore") else 0):
                        gi = 4 * it + sub
                        dst = AP(g_dram.tensor, 1 + GROW * gi,
                                 [[72, 128], [HSTR, H], [1, 72]])
                        tgs.append(E("sync", "dma_start", dst, gs[:, sub],
                                     dma=True,
                                     deps=[ev for s, ev in ev_g if s == sub]))
                    all_st += tgs
                    qkv_done[it] = [None, None, tvs]
                    g_done[it] = tgs
                    st_free[par] = all_st

        sch.barrier()

        if upto >= 2:
            with ExitStack() as sctx:
                q_g = [sctx.enter_context(nc.sbuf_tensor(f"q_g{i}", [128, 8, 128], BF16))
                       for i in range(2)]
                k_g = [sctx.enter_context(nc.sbuf_tensor(f"k_g{i}", [128, 8, 192], BF16))
                       for i in range(2)]
                v_g0 = [sctx.enter_context(nc.sbuf_tensor(f"v_g0{i}", [128, HID], BF16))
                        for i in range(2)]
                v_g1 = [sctx.enter_context(nc.sbuf_tensor(f"v_g1{i}", [64, HID], BF16))
                        for i in range(2)]
                bbt = [sctx.enter_context(nc.sbuf_tensor(f"bb{i}", [128, H * 192], F32))
                       for i in range(2)]
                logt = [sctx.enter_context(nc.sbuf_tensor(f"logt{i}", [128, H * 192], F32))
                        for i in range(2)]
                tvt = [sctx.enter_context(nc.sbuf_tensor(f"tvt{i}", [128, H * 192], F32))
                       for i in range(2)]
                probt = [sctx.enter_context(nc.sbuf_tensor(f"prob{i}", [128, H, 192], BF16))
                         for i in range(2)]
                pTt = [sctx.enter_context(nc.sbuf_tensor(f"pT{i}", [128, H, 256], BF16))
                       for i in range(2)]
                srowt = [sctx.enter_context(nc.sbuf_tensor(f"srow{i}", [128, 64], F32))
                         for i in range(2)]
                atst = [sctx.enter_context(nc.sbuf_tensor(f"atst{i}", [128, 8, 128], BF16))
                        for i in range(2)]
                fr = {k2: [None, None] for k2 in
                      ("qk", "v", "bb", "logt", "tv", "srow", "prob", "ast")}
                st2 = [None, None]
                t_bbinit = [E("gpsimd", "memset", bbt[i][:], -1.0e9)
                            for i in range(2)]
                for g in range(NG):
                    par = g & 1
                    b, gp = g // NGS, g % NGS
                    qg, kg, v0, v1 = q_g[par], k_g[par], v_g0[par], v_g1[par]
                    bb, lg, tv, prob = bbt[par], logt[par], tvt[par], probt[par]
                    pT, srow, ast = pTt[par], srowt[par], atst[par]
                    t_q = E("sync", "dma_start", qg[:],
                            q_dram[:, :, 128 * g:128 * g + 128], dma=True,
                            deps=[fr["qk"][par]])
                    t_k = E("sync", "dma_start", kg[:],
                            k_dram[:, :, KPAD * b + 128 * gp:KPAD * b + 128 * gp + 192],
                            dma=True, deps=[fr["qk"][par]])
                    t_v0 = E("sync", "dma_start", v0[:],
                             v_dram[KPAD * b + 128 * gp:KPAD * b + 128 * gp + 128, :],
                             dma=True, deps=[fr["v"][par]])
                    t_v1 = E("sync", "dma_start", v1[:],
                             v_dram[KPAD * b + 128 * gp + 128:KPAD * b + 128 * gp + 192, :],
                             dma=True, deps=[fr["v"][par]])
                    bb3 = bb.rearrange("p (h c) -> p h c", c=192)
                    t_bbs = []
                    for m in range(4):
                        srcap = AP(g_dram.tensor, GROW * g + 2304 * m + 1,
                                   [[71, 32], [HSTR, H], [1, 71]])
                        t_bbs.append(E(
                            "gpsimd", "dma_start",
                            bb3[32 * m:32 * m + 32, :, 32 * m + 1:32 * m + 72],
                            srcap, dma=True,
                            deps=[fr["bb"][par], t_bbinit[par]]))
                    t_bb = t_bbs[-1]

                    AC_ORDER = [0, 4, 1, 5, 2, 6, 3, 7]
                    CH_ORDER = [0, 2, 1, 3]
                    SIDX = {}
                    for ci, c in enumerate(CH_ORDER):
                        for k2 in range(4):
                            SIDX[4 * c + k2] = 4 * ci + k2
                    t_log = [None] * 8
                    ac_last = None
                    for pos, i2 in enumerate(AC_ORDER):
                        ha, hb_ = QR_ORDER[2 * i2], QR_ORDER[2 * i2 + 1]
                        bk = pos % 4
                        base = 64 * (ha & 1)
                        deps0 = [t_q, t_k, pb_free[bk]]
                        m_a = E("tensor", "matmul", pb[bk][:, 0:192],
                                qg[base:base + 64, ha >> 1, :],
                                kg[base:base + 64, ha >> 1, :],
                                start=True, stop=True, deps=deps0)
                        m_b = E("tensor", "matmul", pb[bk][:, 192:384],
                                qg[base:base + 64, hb_ >> 1, :],
                                kg[base:base + 64, hb_ >> 1, :],
                                start=True, stop=True)
                        ac_last = m_b
                        ev = E("vector", "scalar_tensor_tensor",
                               lg[:, 384 * i2:384 * i2 + 384],
                               pb[bk][:, 0:384], 1.0 / SOFTCAP,
                               bb[:, 384 * i2:384 * i2 + 384],
                               ALU.mult, ALU.add,
                               deps=[m_b, fr["logt"][par]] + t_bbs)
                        pb_free[bk] = ev
                        t_log[i2] = ev
                    fr["qk"][par] = ac_last
                    fr["bb"][par] = t_log[AC_ORDER[-1]]
                    t_ex = [None] * H
                    t_rc = [None, None]
                    t_pr = [None] * H
                    last_pr = None
                    for ci, c in enumerate(CH_ORDER):
                        t_th = E("scalar", "activation",
                                 tv[:, 768 * c:768 * c + 768],
                                 lg[:, 768 * c:768 * c + 768], AF.Tanh,
                                 deps=[t_log[2 * c], t_log[2 * c + 1],
                                       fr["tv"][par]])
                        for k2 in range(4):
                            s = 4 * c + k2
                            t_ex[s] = E("scalar", "activation",
                                        lg[:, 192 * s:192 * s + 192],
                                        tv[:, 192 * s:192 * s + 192], AF.Exp,
                                        scale=SOFTCAP,
                                        accum_out=srow[:, 32 + SIDX[s]:33 + SIDX[s]],
                                        deps=[t_th, fr["srow"][par]])
                        if ci == 1 or ci == 3:
                            h0 = 8 * (ci // 2)
                            t_rc[ci // 2] = E(
                                "vector", "reciprocal",
                                srow[:, 48 + h0:56 + h0],
                                srow[:, 32 + h0:40 + h0],
                                deps=[t_ex[4 * CH_ORDER[ci - 1] + k2]
                                      for k2 in range(4)]
                                + [t_ex[4 * c + k2] for k2 in range(4)]
                                + [fr["srow"][par]])
                            for cc in (CH_ORDER[ci - 1], c):
                                for k2 in range(4):
                                    s = 4 * cc + k2
                                    t_pr[s] = E(
                                        "vector", "tensor_scalar_mul",
                                        prob[:, s, :],
                                        lg[:, 192 * s:192 * s + 192],
                                        srow[:, 48 + SIDX[s]:49 + SIDX[s]],
                                        deps=[t_rc[ci // 2], t_ex[s],
                                              fr["prob"][par]])
                                    last_pr = t_pr[s]
                    fr["logt"][par] = last_pr
                    fr["tv"][par] = t_ex[4 * CH_ORDER[-1] + 3]
                    fr["srow"][par] = last_pr
                    t_pT = [None] * 8
                    tr_last = None
                    for pos, s2_ in enumerate(AC_ORDER):
                        bk = 4 + (pos & 1)
                        mms = []
                        for j in range(2):
                            s = 2 * s2_ + j
                            mms.append(E("tensor", "matmul",
                                         pb[bk][:, 256 * j:256 * j + 128],
                                         prob[:, s, 0:128], ident[:],
                                         start=True, stop=True,
                                         deps=([t_pr[s], t_id, pb_free[bk]]
                                               if j == 0 else [t_pr[s]])))
                            mms.append(E("tensor", "matmul",
                                         pb[bk][0:64, 256 * j + 128:256 * j + 256],
                                         prob[:, s, 128:192], ident[:],
                                         start=True, stop=True))
                        tr_last = mms[-1]
                        ev = E("scalar", "activation", pT[:, 2 * s2_:2 * s2_ + 2, :],
                               pb[bk][:, 0:512], AF.Copy, deps=[mms[-1]])                             if (pos & 2) else                             E("vector", "tensor_copy", pT[:, 2 * s2_:2 * s2_ + 2, :],
                              pb[bk][:, 0:512], deps=[mms[-1]])
                        pb_free[bk] = ev
                        t_pT[s2_] = ev
                    fr["prob"][par] = tr_last
                    t_avev = []
                    av_last = None
                    for hp in range(8):
                        bk = 6 + (hp & 1)
                        mlast = None
                        for h01 in range(2):
                            h = 2 * hp + h01
                            s = QR_SLOT[h]
                            tp = (0, 64) if h01 else None
                            pev = t_pT[s >> 1]
                            E("tensor", "matmul",
                              pb[bk][64 * h01:64 * h01 + 64, 0:128],
                              v0[:, 64 * h:64 * h + 64], pT[:, s, 0:128],
                              start=True, stop=False,
                              deps=([t_v0, pev, pb_free[bk]] if h01 == 0
                                    else [t_v0, pev]),
                              inc=False, tile_position=tp)
                            mlast = E("tensor", "matmul",
                                      pb[bk][64 * h01:64 * h01 + 64, 0:128],
                                      v1[0:64, 64 * h:64 * h + 64],
                                      pT[0:64, s, 128:256],
                                      start=False, stop=True,
                                      deps=[t_v1], tile_position=tp)
                        av_last = mlast
                        ev = E("scalar", "activation", ast[:, hp, :],
                               pb[bk][:, 0:128], AF.Copy,
                               deps=[mlast, st2[par]]) if (hp & 2) else                             E("vector", "tensor_copy", ast[:, hp, :],
                              pb[bk][:, 0:128], deps=[mlast, st2[par]])
                        pb_free[bk] = ev
                        t_avev.append(ev)
                    fr["v"][par] = av_last
                    t_ast = E("scalar", "dma_start",
                              attn_dram[:, :, 128 * g:128 * g + 128], ast[:],
                              dma=True, deps=t_avev)
                    st2[par] = t_ast

        sch.barrier()

        def emit_ln_stage(wname, w_dram, in_dram, in_is_hp8, res_dram, scale_v,
                          out_f32_dram, out_is_ext, tag, prefetch=()):
            ICN = 8 if in_is_hp8 else 32
            with ExitStack() as sctx:
                w_bf = sctx.enter_context(
                    nc.sbuf_tensor(f"w_{tag}", [128, ICN, HID], BF16))
                ing = [sctx.enter_context(
                    nc.sbuf_tensor(f"in_{tag}{i}", [128, ICN, TT], BF16))
                    for i in range(2)]
                zt = [sctx.enter_context(
                    nc.sbuf_tensor(f"z_{tag}{i}", [128, 8, TT], F32))
                    for i in range(3)]
                zbf = sctx.enter_context(nc.sbuf_tensor(f"zb_{tag}", [128, 2, TT], BF16))
                zsq = sctx.enter_context(nc.sbuf_tensor(f"zq_{tag}", [128, 2, TT], BF16))
                mu_t = sctx.enter_context(nc.sbuf_tensor(f"mu_{tag}", [128, TT], F32))
                e2_t = sctx.enter_context(nc.sbuf_tensor(f"e2_{tag}", [128, TT], F32))
                m2_t = sctx.enter_context(nc.sbuf_tensor(f"m2_{tag}", [128, TT], F32))

                t_w = E("gpsimd", "dma_start", w_bf[:],
                        w_dram.rearrange("(ic p) o -> p ic o", p=128), dma=True)
                pf_toks = [E("gpsimd", "dma_start", dst, src, dma=True)
                           for dst, src in prefetch]
                stf = [None, None, None]
                mm_free = [None, None]
                pend = None
                prev_ms = [None, None]
                prev_mq = [None, None]
                stats_free = None
                for it in range(NT + 1):
                    newpend = None
                    if it < NT:
                        par = it & 1
                        j = it % 3
                        b, t0 = it // 4, TT * (it % 4)
                        tk0 = 2048 * b + t0
                        ig = ing[par]
                        t_in = E("sync", "dma_start", ig[:],
                                 in_dram[:, :, tk0:tk0 + TT], dma=True,
                                 deps=[mm_free[par]])
                        if res_dram is x_fm:
                            t_res = E("sync", "dma_start", zt[j][:],
                                      x_fm[b].rearrange("(ic p) t -> p ic t", p=128)
                                      [:, :, t0:t0 + TT], dma=True,
                                      deps=[stf[j]])
                        else:
                            t_res = E("sync", "dma_start", zt[j][:],
                                      res_dram[:, :, tk0:tk0 + TT], dma=True,
                                      deps=[stf[j]])
                        t_z = []
                        for oc in range(8):
                            bk = oc & 1
                            mt = None
                            for ic in range(ICN):
                                mt = E("tensor", "matmul", pb[bk][:, 0:TT],
                                       w_bf[:, ic, 128 * oc:128 * oc + 128],
                                       ig[:, ic, :],
                                       start=(ic == 0), stop=(ic == ICN - 1),
                                       deps=[t_in, t_w, pb_free[bk]] if ic == 0 else [],
                                       inc=(ic == ICN - 1))
                            if scale_v is None:
                                ev = E("vector", "tensor_add", zt[j][:, oc, :],
                                       pb[bk][:, 0:TT], zt[j][:, oc, :],
                                       deps=[mt, t_res])
                            else:
                                ev = E("vector", "scalar_tensor_tensor",
                                       zt[j][:, oc, :],
                                       pb[bk][:, 0:TT], scale_v[:, oc:oc + 1],
                                       zt[j][:, oc, :], ALU.mult, ALU.add,
                                       deps=[mt, t_res])
                            pb_free[bk] = ev
                            t_z.append(ev)
                        mm_free[par] = mt
                        newpend = (it, j, t_z)
                    if pend is not None:
                        it0, j0, t_z = pend
                        b0, t00 = it0 // 4, TT * (it0 % 4)
                        tk00 = 2048 * b0 + t00
                        bkS, bkQ = (2, 3) if (it0 & 1) == 0 else (4, 5)
                        m_s = None
                        m_q = None
                        for oc in range(8):
                            t_cb = E("scalar", "activation", zbf[:, oc & 1, :],
                                     zt[j0][:, oc, :], AF.Copy,
                                     deps=[t_z[oc], prev_ms[oc & 1]])
                            t_cq = E("scalar", "activation", zsq[:, oc & 1, :],
                                     zt[j0][:, oc, :],
                                     AF.Square, deps=[t_z[oc], prev_mq[oc & 1]])
                            m_s = E("tensor", "matmul", pb[bkS][:, 0:TT],
                                    ones128[:], zbf[:, oc & 1, :],
                                    start=(oc == 0), stop=(oc == 7),
                                    deps=[t_cb, t_o128] + (
                                        [pb_free[bkS]] if oc == 0 else []),
                                    inc=True)
                            m_q = E("tensor", "matmul", pb[bkQ][:, 0:TT],
                                    ones128[:], zsq[:, oc & 1, :],
                                    start=(oc == 0), stop=(oc == 7),
                                    deps=[t_cq, t_o128] + (
                                        [pb_free[bkQ]] if oc == 0 else []),
                                    inc=True)
                            prev_ms[oc & 1] = m_s
                            prev_mq[oc & 1] = m_q
                        t_mu = E("vector", "tensor_scalar_mul", mu_t[:],
                                 pb[bkS][:, 0:TT], 1.0 / HID,
                                 deps=[m_s, stats_free])
                        pb_free[bkS] = t_mu
                        t_e2 = E("vector", "tensor_scalar_mul", e2_t[:],
                                 pb[bkQ][:, 0:TT], 1.0 / HID,
                                 deps=[m_q, stats_free])
                        pb_free[bkQ] = t_e2
                        t_m2 = E("vector", "tensor_tensor", m2_t[:],
                                 mu_t[:], mu_t[:], ALU.mult, deps=[t_mu])
                        t_var = E("vector", "tensor_sub", e2_t[:],
                                  e2_t[:], m2_t[:], deps=[t_e2, t_m2])
                        t_sd = E("scalar", "activation", m2_t[:], e2_t[:],
                                 AF.Ln, bias=epsv[:, 0:1],
                                 deps=[t_var, t_eps])
                        t_rs = E("scalar", "activation", e2_t[:], m2_t[:],
                                 AF.Exp, scale=-0.5,
                                 deps=[t_sd])
                        t_mr = E("vector", "tensor_tensor", mu_t[:],
                                 mu_t[:], e2_t[:], ALU.mult, deps=[t_rs])
                        t_o = []
                        for oc in range(8):
                            t_t1 = E("vector", "tensor_mul", zt[j0][:, oc, :],
                                     zt[j0][:, oc, :], e2_t[:],
                                     deps=[t_rs, t_z[oc]])
                            t_t2 = E("vector", "tensor_sub", zt[j0][:, oc, :],
                                     zt[j0][:, oc, :], mu_t[:],
                                     deps=[t_mr, t_t1])
                            t_o.append(t_t2)
                        stats_free = t_o[-1]
                        if out_is_ext:
                            st = E("gpsimd", "dma_start",
                                   out_fm[b0].rearrange(
                                       "(oc p) t -> p oc t", p=128)
                                   [:, :, t00:t00 + TT],
                                   zt[j0][:], dma=True, deps=t_o)
                        else:
                            st = E("gpsimd", "dma_start",
                                   out_f32_dram[:, :, tk00:tk00 + TT],
                                   zt[j0][:], dma=True, deps=t_o)
                        stf[j0] = st
                    pend = newpend
                return pf_toks

        s4ctx = ExitStack()
        if upto >= 4:
            w1b = s4ctx.enter_context(nc.sbuf_tensor("w1b", [128, 8, DH], BF16))
        if upto >= 3:
            pf = emit_ln_stage(
                "post", wpostT, attn_dram, True, x_fm, None,
                x1f_dram, False, "p",
                prefetch=([(w1b[:], w1T.rearrange("(ic p) o -> p ic o", p=128))]
                          if upto >= 4 else ()))
        else:
            pf = []

        sch.barrier()

        if upto >= 4:
            with s4ctx as sctx:
                w2b = sctx.enter_context(nc.sbuf_tensor("w2b", [128, 8, DH], BF16))
                x1g = [sctx.enter_context(nc.sbuf_tensor(f"x1g{i}", [128, 8, TT], BF16))
                       for i in range(2)]
                sil = sctx.enter_context(nc.sbuf_tensor("sil", [128, TT], F32))
                ast = sctx.enter_context(nc.sbuf_tensor("a_st", [128, 32, TT], BF16))
                t_w1 = pf[0] if pf else E(
                    "gpsimd", "dma_start", w1b[:],
                    w1T.rearrange("(ic p) o -> p ic o", p=128), dma=True)
                t_w2 = E("gpsimd", "dma_start", w2b[:],
                         w2T.rearrange("(ic p) o -> p ic o", p=128), dma=True)
                stf = [None, None]
                mm_free = [None, None]
                st_half = [None, None]
                for it in range(NT):
                    par = it & 1
                    b, t0 = it // 4, TT * (it % 4)
                    tk0 = 2048 * b + t0
                    xg = x1g[par]
                    t_in = E("gpsimd", "dma_start", xg[:],
                             x1f_dram[:, :, tk0:tk0 + TT], dma=True,
                             deps=[mm_free[par]])
                    t_as = []
                    prev_stt = None
                    for oc in range(32):
                        bk0, bk1 = (0, 1) if (oc & 1) == 0 else (2, 3)
                        m1 = None
                        for ic in range(8):
                            m1 = E("tensor", "matmul", pb[bk0][:, 0:TT],
                                   w1b[:, ic, 128 * oc:128 * oc + 128], xg[:, ic, :],
                                   start=(ic == 0), stop=(ic == 7),
                                   deps=[t_in, t_w1, pb_free[bk0]] if ic == 0 else [],
                                   inc=(ic == 7))
                        t_si = E("scalar", "activation", sil[:], pb[bk0][:, 0:TT],
                                 AF.Silu, scale=s1v[:, oc:oc + 1],
                                 deps=[m1, t_s1, prev_stt])
                        pb_free[bk0] = t_si
                        m2 = None
                        for ic in range(8):
                            m2 = E("tensor", "matmul", pb[bk1][:, 0:TT],
                                   w2b[:, ic, 128 * oc:128 * oc + 128], xg[:, ic, :],
                                   start=(ic == 0), stop=(ic == 7),
                                   deps=[t_in, t_w2, pb_free[bk1]] if ic == 0 else [],
                                   inc=(ic == 7))
                        ev = E("vector", "scalar_tensor_tensor", ast[:, oc, :],
                               pb[bk1][:, 0:TT], s2v[:, oc:oc + 1], sil[:],
                               ALU.mult, ALU.mult,
                               deps=[m2, t_si, t_s2, st_half[oc // 16]],
                               )
                        pb_free[bk1] = ev
                        prev_stt = ev
                        t_as.append(ev)
                        if oc == 15:
                            st_half[0] = E("scalar", "dma_start",
                                           a_dram[:, 0:16, tk0:tk0 + TT],
                                           ast[:, 0:16, :], dma=True,
                                           deps=t_as[0:16])
                    st_half[1] = E("scalar", "dma_start",
                                   a_dram[:, 16:32, tk0:tk0 + TT],
                                   ast[:, 16:32, :], dma=True, deps=t_as[16:32])
                    stf[par] = st_half[1]
                    mm_free[par] = m2

        sch.barrier()

        def emit_h_stage(hT, hs_v, in_dram, out_dram, tag, bufs,
                         t_pn0=None, next_hT=None):
            NP, PW = 8, 512
            POC = PW // 128
            if True:
                panel, ag, bst = bufs
                stf = [None, None]
                mm_free = [None, None]
                pn_free = [None, None]
                t_pn = [None, None]
                t_pn[0] = t_pn0 if t_pn0 is not None else E(
                    "gpsimd", "dma_start", panel[0][:],
                    hT.rearrange("(ic p) o -> p ic o", p=128)
                    [:, :, 0:PW], dma=True)
                for P in range(NP):
                    pp = P & 1
                    if P + 1 < NP:
                        t_pn[1 - pp] = E("gpsimd", "dma_start", panel[1 - pp][:],
                                         hT.rearrange("(ic p) o -> p ic o", p=128)
                                         [:, :, PW * (P + 1):PW * (P + 2)],
                                         dma=True, deps=[pn_free[1 - pp]])
                    last_mm = None
                    for it in range(NT):
                        par = it & 1
                        b, t0 = it // 4, TT * (it % 4)
                        tk0 = 2048 * b + t0
                        a_in = ag[par]
                        t_in = E("sync", "dma_start", a_in[:],
                                 in_dram[:, :, tk0:tk0 + TT], dma=True,
                                 deps=[mm_free[par]])
                        evs = []
                        for oc in range(POC):
                            bk = oc & 3
                            mt = None
                            for ic in range(32):
                                mt = E("tensor", "matmul", pb[bk][:, 0:TT],
                                       panel[pp][:, ic, 128 * oc:128 * oc + 128],
                                       a_in[:, ic, :],
                                       start=(ic == 0), stop=(ic == 31),
                                       deps=([t_in, t_pn[pp], pb_free[bk]]
                                             if ic == 0 else []),
                                       inc=(ic == 31))
                            ev = E("scalar", "activation", bst[par][:, oc, :],
                                   pb[bk][:, 0:TT], AF.Gelu,
                                   scale=hs_v[:, POC * P + oc:POC * P + oc + 1],
                                   deps=[mt, stf[par]])
                            pb_free[bk] = ev
                            evs.append(ev)
                            last_mm = mt
                        mm_free[par] = last_mm
                        stf[par] = E("scalar", "dma_start",
                                     out_dram[:, POC * P:POC * P + POC,
                                              tk0:tk0 + TT],
                                     bst[par][:], dma=True, deps=evs)
                    pn_free[pp] = last_mm
                if next_hT is not None:
                    return E("gpsimd", "dma_start", panel[0][:],
                             next_hT.rearrange("(ic p) o -> p ic o", p=128)
                             [:, :, 0:PW], dma=True, deps=[pn_free[0]])
                return None

        if upto >= 5 or upto >= 6:
            with ExitStack() as hctx:
                PW_ = 512
                hbufs = (
                    [hctx.enter_context(
                        nc.sbuf_tensor(f"pn_h{i}", [128, 32, PW_], BF16))
                     for i in range(2)],
                    [hctx.enter_context(
                        nc.sbuf_tensor(f"ag_h{i}", [128, 32, TT], BF16))
                     for i in range(2)],
                    [hctx.enter_context(
                        nc.sbuf_tensor(f"bs_h{i}", [128, PW_ // 128, TT], BF16))
                     for i in range(2)],
                )
                t_pn2 = None
                if upto >= 5:
                    t_pn2 = emit_h_stage(h1T, h1v, a_dram, b_dram, "h1", hbufs,
                                         next_hT=(h2T if upto >= 6 else None))
                sch.barrier()
                if upto >= 6:
                    emit_h_stage(h2T, h2v, b_dram, c_dram, "h2", hbufs,
                                 t_pn0=t_pn2)
        sch.barrier()

        if upto >= 7:
            emit_ln_stage("w3", w3T, c_dram, False, x1f_dram, w3v,
                          None, True, "y")

        sch.flush()
        sch.final_wait()

    return nc



def kernel(L0_attn_hidden_states, L0_attn_position_embeddings, L0_attn_w_q,
           L0_attn_w_k, L0_attn_w_v, L0_attn_w_rel, L0_attn_w_post,
           L0_attn_per_dim_scale, L0_ffn_w1_w, L0_ffn_w1_s, L0_ffn_w2_w,
           L0_ffn_w2_s, L0_ffn_h1_w, L0_ffn_h1_s, L0_ffn_h2_w, L0_ffn_h2_s,
           L0_ffn_w3_w, L0_ffn_w3_s, _upto=99, _dump=(), _nc=None):
    f32 = np.float32
    x = np.asarray(L0_attn_hidden_states, f32)
    B = x.shape[0]
    n_cores = 8
    bpc = B // n_cores

    def T(a):
        return np.ascontiguousarray(np.asarray(a, f32).T)

    common = {
        "wqT": T(L0_attn_w_q), "wkT": T(L0_attn_w_k), "wvT": T(L0_attn_w_v),
        "wrelT": T(L0_attn_w_rel), "wpostT": T(L0_attn_w_post),
        "posT": T(L0_attn_position_embeddings),
        "pds": np.ascontiguousarray(np.asarray(L0_attn_per_dim_scale, f32)),
        "w1T": T(L0_ffn_w1_w), "w2T": T(L0_ffn_w2_w),
        "h1T": T(L0_ffn_h1_w), "h2T": T(L0_ffn_h2_w), "w3T": T(L0_ffn_w3_w),
        "s1": np.ascontiguousarray(np.asarray(L0_ffn_w1_s, f32).reshape(32, 128).T),
        "s2": np.ascontiguousarray(np.asarray(L0_ffn_w2_s, f32).reshape(32, 128).T),
        "h1s": np.ascontiguousarray(np.asarray(L0_ffn_h1_s, f32).reshape(32, 128).T),
        "h2s": np.ascontiguousarray(np.asarray(L0_ffn_h2_s, f32).reshape(32, 128).T),
        "w3s": np.ascontiguousarray(np.asarray(L0_ffn_w3_s, f32).reshape(8, 128).T),
    }
    x_fm = np.ascontiguousarray(x.transpose(0, 2, 1))
    in_maps = []
    for c in range(n_cores):
        m = dict(common)
        m["x_fm"] = np.ascontiguousarray(x_fm[c * bpc:(c + 1) * bpc])
        in_maps.append(m)

    nc = _nc if _nc is not None else build_program(upto=_upto, dump=_dump)
    res = run_bass_kernel_spmd(nc, in_maps, core_ids=list(range(n_cores)))
    if _dump:
        return res
    out = np.concatenate([r["out_fm"] for r in res.results], axis=0)
    return np.ascontiguousarray(out.transpose(0, 2, 1))


# version 12
# speedup vs baseline: 1.0553x; 1.0553x over previous
import math
import os
from contextlib import ExitStack
S1_MODE = os.environ.get("K_S1_MODE", "full")
S1_NT = int(os.environ.get("K_S1_NT", "999"))
S1_QRH = int(os.environ.get("K_S1_QRH", "16"))

import numpy as np

import concourse.bass as bass
import concourse.mybir as mybir
from concourse.bass_utils import run_bass_kernel_spmd
from concourse.ap import AP

F32 = mybir.dt.float32
BF16 = mybir.dt.bfloat16
AF = mybir.ActivationFunctionType
ALU = mybir.AluOpType
AX = mybir.AxisListType

H = 16
D = 64
HID = 1024
DH = 4096
CHUNK = 32
PAST, FUT = 31, 8
CTX = 71
SEQ = 2048
NSEQ = 4
NTOK = NSEQ * SEQ
TT = 512
NT = NTOK // TT
NGS = SEQ // 128
NG = NSEQ * NGS
KPAD = SEQ + 64
GROW = 72 * 128
HSTR = 1 + NTOK * 72 + 7

QR_ORDER = [0, 2, 4, 6, 8, 10, 12, 14, 1, 3, 5, 7, 9, 11, 13, 15]
QR_SLOT = {h: i for i, h in enumerate(QR_ORDER)}

Q_SCALE = D ** (-0.5) / math.log(2)
K_SCALE = math.log(1.0 + math.e) / math.log(2)
SOFTCAP = 50.0

STRICT_SAME_ENGINE = True

ENGINES = ("tensor", "vector", "scalar", "gpsimd", "sync")
DMA_QUEUES = ("gpsimd", "sync", "scalar")
NDMA_SLOTS = 4


class Sched:

    def __init__(self, nc, ctx, block):
        self.nc = nc
        self.block = block
        names = list(ENGINES)
        for q in DMA_QUEUES:
            for s in range(NDMA_SLOTS):
                names.append(f"dq_{q}{s}")
        self.sems = {n: ctx.enter_context(nc.semaphore(f"s_{n}")) for n in names}
        self.count = {n: 0 for n in names}
        self.queues = {e: [] for e in ENGINES}
        self.dma_uses = {q: 0 for q in DMA_QUEUES}
        self.last_wait = {e: {n: 0 for n in names} for e in ENGINES}
        self.pending_barrier = {e: {} for e in ENGINES}

    def emit(self, eng, fn_name, *args, deps=(), inc=True, dma=False, **kwargs):
        if dma:
            slot = self.dma_uses[eng] % NDMA_SLOTS
            self.dma_uses[eng] += 1
            sem_name = f"dq_{eng}{slot}"
        else:
            sem_name = eng
        waits = dict(self.pending_barrier[eng])
        self.pending_barrier[eng] = {}
        if dma and self.count[sem_name] > 0:
            if self.count[sem_name] > waits.get(sem_name, 0):
                waits[sem_name] = self.count[sem_name]
        for tok in deps:
            if tok is None:
                continue
            n, v = tok
            if n == eng and not STRICT_SAME_ENGINE:
                continue
            if v > waits.get(n, 0):
                waits[n] = v
        real_waits = []
        for n, v in waits.items():
            if v > self.last_wait[eng][n]:
                real_waits.append((n, v))
                self.last_wait[eng][n] = v
        tok = None
        if inc:
            self.count[sem_name] += 16 if dma else 1
            tok = (sem_name, self.count[sem_name])

        def step(engobj, fn_name=fn_name, args=args, kwargs=kwargs,
                 real_waits=real_waits, inc=inc, dma=dma, sem_name=sem_name):
            for n, v in real_waits:
                engobj.wait_ge(self.sems[n], v)
            inst = getattr(engobj, fn_name)(*args, **kwargs)
            if inc:
                inst.then_inc(self.sems[sem_name], 16 if dma else 1)

        self.queues[eng].append(step)
        return tok

    def raw(self, eng, fn):
        self.queues[eng].append(fn)

    def barrier(self):
        snap = {n: v for n, v in self.count.items() if v > 0}
        for e in ENGINES:
            merged = dict(self.pending_barrier[e])
            for n, v in snap.items():
                if v > merged.get(n, 0):
                    merged[n] = v
            self.pending_barrier[e] = merged

    def flush(self):
        block = self.block
        for e in ENGINES:
            q = self.queues[e]
            if not q:
                continue
            self.queues[e] = []
            section = getattr(block, e)

            def body(engobj, q=q):
                for step in q:
                    step(engobj)

            section(body)

    def final_wait(self):
        snap = {n: v for n, v in self.count.items() if v > 0}

        def body(engobj):
            for n, v in snap.items():
                engobj.wait_ge(self.sems[n], v)

        self.block.sync(body)


def build_program(upto=99, dump=()):
    nc = bass.Bass("TRN2", target_bir_lowering=False, debug=False)

    def dram(name, shape, dtype, kind="Internal"):
        k = "ExternalOutput" if name in dump else kind
        return nc.dram_tensor(name, shape, dtype, kind=k).ap()

    x_fm = nc.dram_tensor("x_fm", [NSEQ, HID, SEQ], F32, kind="ExternalInput").ap()
    wqT = nc.dram_tensor("wqT", [HID, HID], F32, kind="ExternalInput").ap()
    wkT = nc.dram_tensor("wkT", [HID, HID], F32, kind="ExternalInput").ap()
    wvT = nc.dram_tensor("wvT", [HID, HID], F32, kind="ExternalInput").ap()
    wrelT = nc.dram_tensor("wrelT", [HID, HID], F32, kind="ExternalInput").ap()
    wpostT = nc.dram_tensor("wpostT", [HID, HID], F32, kind="ExternalInput").ap()
    posT = nc.dram_tensor("posT", [HID, CTX], F32, kind="ExternalInput").ap()
    pdsd = nc.dram_tensor("pds", [D], F32, kind="ExternalInput").ap()
    w1T = nc.dram_tensor("w1T", [HID, DH], F32, kind="ExternalInput").ap()
    w2T = nc.dram_tensor("w2T", [HID, DH], F32, kind="ExternalInput").ap()
    h1T = nc.dram_tensor("h1T", [DH, DH], F32, kind="ExternalInput").ap()
    h2T = nc.dram_tensor("h2T", [DH, DH], F32, kind="ExternalInput").ap()
    w3T = nc.dram_tensor("w3T", [DH, HID], F32, kind="ExternalInput").ap()
    s1d = nc.dram_tensor("s1", [128, 32], F32, kind="ExternalInput").ap()
    s2d = nc.dram_tensor("s2", [128, 32], F32, kind="ExternalInput").ap()
    h1sd = nc.dram_tensor("h1s", [128, 32], F32, kind="ExternalInput").ap()
    h2sd = nc.dram_tensor("h2s", [128, 32], F32, kind="ExternalInput").ap()
    w3sd = nc.dram_tensor("w3s", [128, 8], F32, kind="ExternalInput").ap()

    out_fm = nc.dram_tensor("out_fm", [NSEQ, HID, SEQ], F32,
                            kind="ExternalOutput").ap()

    q_dram = dram("q_dram", [128, 8, NTOK], BF16)
    k_dram = dram("k_dram", [128, 8, NSEQ * KPAD], BF16)
    v_dram = dram("v_dram", [NSEQ * KPAD, HID], BF16)
    g_dram = dram("g_dram", [H, HSTR], F32)
    attn_dram = dram("attn_dram", [128, 8, NTOK], BF16)
    x1f_dram = dram("x1f_dram", [128, 8, NTOK], F32)
    a_dram = dram("a_dram", [128, 32, NTOK], BF16)
    b_dram = dram("b_dram", [128, 32, NTOK], BF16)
    c_dram = dram("c_dram", [128, 32, NTOK], BF16)

    ctx0 = ExitStack()
    with ctx0:
        block = ctx0.enter_context(nc.Block())
        sch = Sched(nc, ctx0, block)

        ident = ctx0.enter_context(nc.sbuf_tensor("ident", [128, 128], BF16))
        sq128 = ctx0.enter_context(nc.sbuf_tensor("sq128", [128, 8], F32))
        relk = ctx0.enter_context(nc.sbuf_tensor("relk", [128, 8, 72], BF16))
        ones_bf = ctx0.enter_context(nc.sbuf_tensor("ones_bf", [128, 2], BF16))
        ones128 = ctx0.enter_context(nc.sbuf_tensor("ones128", [128, 128], BF16))
        ones_f1 = ctx0.enter_context(nc.sbuf_tensor("ones_f1", [1, 128], F32))
        pb = [ctx0.enter_context(nc.psum_tensor(f"pb{i}", [128, 512], F32))
              for i in range(8)]
        pb_free = {i: None for i in range(8)}

        s1v = ctx0.enter_context(nc.sbuf_tensor("s1v", [128, 32], F32))
        s2v = ctx0.enter_context(nc.sbuf_tensor("s2v", [128, 32], F32))
        h1v = ctx0.enter_context(nc.sbuf_tensor("h1v", [128, 32], F32))
        h2v = ctx0.enter_context(nc.sbuf_tensor("h2v", [128, 32], F32))
        w3v = ctx0.enter_context(nc.sbuf_tensor("w3v", [128, 8], F32))
        zeros = ctx0.enter_context(nc.sbuf_tensor("zeros", [128, 1024], BF16))
        zerof = ctx0.enter_context(nc.sbuf_tensor("zerof", [1, 16], F32))
        epsv = ctx0.enter_context(nc.sbuf_tensor("epsv", [128, 1], F32))

        E = sch.emit

        for i in range(8):
            pb_free[i] = E("vector", "memset", pb[i][:], 0.0)
        t_id0 = E("gpsimd", "memset", ident[:], 0.0)
        t_id = E("gpsimd", "affine_select", out=ident[:], in_=ident[:],
                 compare_op=ALU.not_equal, fill=1.0, base=0,
                 pattern=[[-1, 128]], channel_multiplier=1, deps=[t_id0])
        t_ones = E("gpsimd", "memset", ones_bf[:], 1.0)
        t_o128 = E("gpsimd", "memset", ones128[:], 1.0)
        t_ones1 = E("gpsimd", "memset", ones_f1[:], 1.0)
        t_zero = E("gpsimd", "memset", zeros[:], 0.0)
        t_zerof = E("gpsimd", "memset", zerof[:], 0.0)
        t_eps = E("gpsimd", "memset", epsv[:], 1e-5)

        t_s1 = E("sync", "dma_start", s1v[:], s1d, dma=True)
        t_s2 = E("sync", "dma_start", s2v[:], s2d, dma=True)
        t_h1 = E("sync", "dma_start", h1v[:], h1sd, dma=True)
        t_h2 = E("sync", "dma_start", h2v[:], h2sd, dma=True)
        t_w3 = E("sync", "dma_start", w3v[:], w3sd, dma=True)

        pds_ap = pdsd.rearrange("(p o) -> p o", o=1)
        t_p0 = E("sync", "dma_start", sq128[0:64, 1:2], pds_ap, dma=True)
        t_p1 = E("sync", "dma_start", sq128[64:128, 1:2], pds_ap, dma=True)
        t_a = E("scalar", "activation", sq128[:, 2:3], sq128[:, 1:2], AF.Abs,
                deps=[t_p0, t_p1])
        t_e = E("scalar", "activation", sq128[:, 3:4], sq128[:, 2:3], AF.Exp,
                scale=-1.0, deps=[t_a])
        t_l = E("scalar", "activation", sq128[:, 4:5], sq128[:, 3:4], AF.Ln,
                bias=1.0, deps=[t_e])
        t_m = E("vector", "tensor_scalar_max", sq128[:, 5:6], sq128[:, 1:2],
                0.0, deps=[t_p0, t_p1])
        t_sp = E("vector", "tensor_add", sq128[:, 6:7], sq128[:, 5:6],
                 sq128[:, 4:5], deps=[t_m, t_l])
        t_sq = E("vector", "tensor_scalar_mul", sq128[:, 0:1], sq128[:, 6:7],
                 Q_SCALE, deps=[t_sp])

        pad_toks = []
        for b in range(NSEQ):
            for off in (0, KPAD - 32):
                pad_toks.append(E("sync", "dma_start",
                                  k_dram[:, :, b * KPAD + off:b * KPAD + off + 32],
                                  zeros[:, 0:256].rearrange("p (a w) -> p a w", a=8),
                                  dma=True, deps=[t_zero]))
                pad_toks.append(E("sync", "dma_start",
                                  v_dram[b * KPAD + off:b * KPAD + off + 32, :],
                                  zeros[0:32, :], dma=True, deps=[t_zero]))
        t_guard = E("sync", "dma_start", g_dram[:, 0:1], zerof[0:1, :],
                    dma=True, deps=[t_zerof], allow_slow_non_contiguous=True)

        if upto >= 1:
            with ExitStack() as sctx:
                wq_bf = sctx.enter_context(nc.sbuf_tensor("wq_bf", [128, 8, HID], BF16))
                wk_bf = sctx.enter_context(nc.sbuf_tensor("wk_bf", [128, 8, HID], BF16))
                wv_bf = sctx.enter_context(nc.sbuf_tensor("wv_bf", [128, 8, HID], BF16))
                wr_bf = sctx.enter_context(nc.sbuf_tensor("wr_bf", [128, 8, HID], BF16))
                posb = sctx.enter_context(nc.sbuf_tensor("posb", [128, 8, 72], BF16))
                xbf = [sctx.enter_context(nc.sbuf_tensor(f"xbf{i}", [128, 8, TT], BF16))
                       for i in range(2)]
                qst = [sctx.enter_context(nc.sbuf_tensor(f"qst{i}", [128, 8, TT], BF16))
                       for i in range(2)]
                kst = [sctx.enter_context(nc.sbuf_tensor(f"kst{i}", [128, 8, TT], BF16))
                       for i in range(2)]
                vst = [sctx.enter_context(nc.sbuf_tensor(f"vst{i}", [128, 4, HID], BF16))
                       for i in range(2)]
                gst = [sctx.enter_context(nc.sbuf_tensor(f"gst{i}", [128, 4, H, 72], F32))
                       for i in range(2)]

                def wload(dst, src):
                    return E("gpsimd", "dma_start", dst[:],
                             src.rearrange("(ic p) o -> p ic o", p=128), dma=True)

                t_wq = wload(wq_bf, wqT)
                t_wk = wload(wk_bf, wkT)
                t_wv = wload(wv_bf, wvT)
                t_wr = wload(wr_bf, wrelT)
                t_posz = E("gpsimd", "memset", posb[:], 0.0)
                t_pos = E("gpsimd", "dma_start", posb[:, :, 0:71],
                          posT.rearrange("(ic p) l -> p ic l", p=128),
                          dma=True, deps=[t_posz])

                t_relk = []
                for oc in range(8 if S1_MODE != "w" else 0):
                    bk = 6 + (oc & 1)
                    mt = None
                    for ic in range(8):
                        mt = E("tensor", "matmul", pb[bk][:, 0:72],
                               wr_bf[:, ic, 128 * oc:128 * oc + 128],
                               posb[:, ic, :],
                               start=(ic == 0), stop=(ic == 7),
                               deps=[t_wr, t_pos, pb_free[bk]] if ic == 0 else [],
                               inc=(ic == 7))
                    ev = E("vector", "tensor_copy", relk[:, oc, :], pb[bk][:, 0:72],
                           deps=[mt])
                    pb_free[bk] = ev
                    t_relk.append(ev)

                qkv_done = [[None] * 3 for _ in range(NT)]
                g_done = [None] * NT
                st_free = [None, None]
                for it in range(min(NT, S1_NT) if S1_MODE not in ("w", "relk") else 0):
                    par = it & 1
                    b, t0 = it // 4, TT * (it % 4)
                    xb, qs, ks, vs, gs = xbf[par], qst[par], kst[par], vst[par], gst[par]
                    free_dep = st_free[par] or []
                    t_x = E("gpsimd", "dma_start", xb[:],
                            x_fm[b].rearrange("(ic p) t -> p ic t", p=128)
                            [:, :, t0:t0 + TT],
                            dma=True, deps=[t_wq] + list(free_dep))
                    ev_q = []
                    ev_k = []
                    for proj, wbf, stg, evl in (
                            ("q", wq_bf, qs, ev_q), ("k", wk_bf, ks, ev_k))[
                                :0 if S1_MODE == "x" else 2]:
                        for oc in range(8):
                            bk = oc & 1
                            mt = None
                            for ic in range(8):
                                mt = E("tensor", "matmul", pb[bk][:, 0:TT],
                                       wbf[:, ic, 128 * oc:128 * oc + 128],
                                       xb[:, ic, :],
                                       start=(ic == 0), stop=(ic == 7),
                                       deps=[t_x, pb_free[bk]] if ic == 0 else [],
                                       inc=(ic == 7))
                            if proj == "q":
                                ev = E("scalar", "activation", stg[:, oc, :],
                                       pb[bk][:, 0:TT], AF.Copy,
                                       scale=sq128[:, 0:1], deps=[mt, t_sq])
                            else:
                                ev = E("scalar", "activation", stg[:, oc, :],
                                       pb[bk][:, 0:TT], AF.Copy,
                                       scale=K_SCALE, deps=[mt])
                            pb_free[bk] = ev
                            evl.append(ev)
                    ev_v = []
                    for sub in range(4 if S1_MODE in ("full", "v", "qr") else 0):
                        for nh in range(2):
                            bk = 2 + nh
                            mt = None
                            for ic in range(8):
                                mt = E("tensor", "matmul", pb[bk][:, 0:512],
                                       xb[:, ic, 128 * sub:128 * sub + 128],
                                       wv_bf[:, ic, 512 * nh:512 * nh + 512],
                                       start=(ic == 0), stop=(ic == 7),
                                       deps=[t_x, t_wv, pb_free[bk]] if ic == 0 else [],
                                       inc=(ic == 7))
                            ev = E("scalar", "activation",
                                   vs[:, sub, 512 * nh:512 * nh + 512],
                                   pb[bk][:, 0:512], AF.Copy, deps=[mt])
                            pb_free[bk] = ev
                            ev_v.append(ev)
                    ev_g = []
                    for sub in range(4 if S1_MODE in ("full", "qr", "qr_nostore", "qr_noevac") else 0):
                        n_qr = min(16, S1_QRH)
                        for q4 in range(4):
                            heads = [h for h in QR_ORDER[4 * q4:4 * q4 + 4]
                                     if h < n_qr or S1_QRH >= 16]
                            if not heads:
                                continue
                            bk = 4 + q4
                            mts = []
                            for j, h in enumerate(heads):
                                mts.append(E(
                                    "tensor", "matmul",
                                    pb[bk][:, 128 * j:128 * j + 72],
                                    qs[64 * (h & 1):64 * (h & 1) + 64,
                                       h >> 1, 128 * sub:128 * sub + 128],
                                    relk[64 * (h & 1):64 * (h & 1) + 64, h >> 1, :],
                                    start=True, stop=True,
                                    deps=([ev_q[h >> 1], t_relk[h >> 1],
                                           pb_free[bk]] if j == 0 else
                                          [ev_q[h >> 1], t_relk[h >> 1]]),
                                    inc=(j == len(heads) - 1)))
                            if S1_MODE == "qr_noevac":
                                pb_free[bk] = mts[-1]
                                continue
                            ev = E("scalar", "activation",
                                   gs[:, sub, 4 * q4:4 * q4 + len(heads), :],
                                   pb[bk][:, 0:512].rearrange(
                                       "p (a w) -> p a w", a=4)[:, 0:len(heads), 0:72],
                                   AF.Copy,
                                   scale=1.0 / SOFTCAP, deps=[mts[-1]])
                            pb_free[bk] = ev
                            ev_g.append((sub, ev))
                    all_st = []
                    if ev_q:
                        tq = E("sync", "dma_start",
                               q_dram[:, :, 2048 * b + t0:2048 * b + t0 + TT],
                               qs[:], dma=True, deps=ev_q)
                        tk = E("sync", "dma_start",
                               k_dram[:, :, KPAD * b + 32 + t0:KPAD * b + 32 + t0 + TT],
                               ks[:], dma=True, deps=ev_k)
                        all_st += [tq, tk]
                    tvs = []
                    for sub in range(4 if ev_v else 0):
                        r0 = KPAD * b + 32 + t0 + 128 * sub
                        tvs.append(E("sync", "dma_start", v_dram[r0:r0 + 128, :],
                                     vs[:, sub, :], dma=True, deps=[ev_v[2 * sub],
                                                                    ev_v[2 * sub + 1]]))
                    all_st += tvs
                    tgs = []
                    for sub in range(4 if (ev_g and S1_MODE != "qr_nostore") else 0):
                        gi = 4 * it + sub
                        dst = AP(g_dram.tensor, 1 + GROW * gi,
                                 [[72, 128], [HSTR, H], [1, 72]])
                        tgs.append(E("sync", "dma_start", dst, gs[:, sub],
                                     dma=True,
                                     deps=[ev for s, ev in ev_g if s == sub]))
                    all_st += tgs
                    qkv_done[it] = [None, None, tvs]
                    g_done[it] = tgs
                    st_free[par] = all_st

        sch.barrier()

        if upto >= 2:
            with ExitStack() as sctx:
                q_g = [sctx.enter_context(nc.sbuf_tensor(f"q_g{i}", [128, 8, 128], BF16))
                       for i in range(2)]
                k_g = [sctx.enter_context(nc.sbuf_tensor(f"k_g{i}", [128, 8, 192], BF16))
                       for i in range(2)]
                v_g0 = [sctx.enter_context(nc.sbuf_tensor(f"v_g0{i}", [128, HID], BF16))
                        for i in range(2)]
                v_g1 = [sctx.enter_context(nc.sbuf_tensor(f"v_g1{i}", [64, HID], BF16))
                        for i in range(2)]
                bbt = [sctx.enter_context(nc.sbuf_tensor(f"bb{i}", [128, H * 192], F32))
                       for i in range(2)]
                logt = [sctx.enter_context(nc.sbuf_tensor(f"logt{i}", [128, H * 192], F32))
                        for i in range(2)]
                tvt = [sctx.enter_context(nc.sbuf_tensor(f"tvt{i}", [128, H * 192], F32))
                       for i in range(2)]
                probt = [sctx.enter_context(nc.sbuf_tensor(f"prob{i}", [128, H, 192], BF16))
                         for i in range(2)]
                pTt = [sctx.enter_context(nc.sbuf_tensor(f"pT{i}", [128, H, 256], BF16))
                       for i in range(2)]
                srowt = [sctx.enter_context(nc.sbuf_tensor(f"srow{i}", [128, 64], F32))
                         for i in range(2)]
                atst = [sctx.enter_context(nc.sbuf_tensor(f"atst{i}", [128, 8, 128], BF16))
                        for i in range(2)]
                fr = {k2: [None, None] for k2 in
                      ("qk", "v", "bb", "logt", "tv", "srow", "prob", "ast")}
                st2 = [None, None]
                t_bbinit = [E("gpsimd", "memset", bbt[i][:], -1.0e9)
                            for i in range(2)]
                for g in range(NG):
                    par = g & 1
                    b, gp = g // NGS, g % NGS
                    qg, kg, v0, v1 = q_g[par], k_g[par], v_g0[par], v_g1[par]
                    bb, lg, tv, prob = bbt[par], logt[par], tvt[par], probt[par]
                    pT, srow, ast = pTt[par], srowt[par], atst[par]
                    t_q = E("sync", "dma_start", qg[:],
                            q_dram[:, :, 128 * g:128 * g + 128], dma=True,
                            deps=[fr["qk"][par]])
                    t_k = E("sync", "dma_start", kg[:],
                            k_dram[:, :, KPAD * b + 128 * gp:KPAD * b + 128 * gp + 192],
                            dma=True, deps=[fr["qk"][par]])
                    t_v0 = E("sync", "dma_start", v0[:],
                             v_dram[KPAD * b + 128 * gp:KPAD * b + 128 * gp + 128, :],
                             dma=True, deps=[fr["v"][par]])
                    t_v1 = E("sync", "dma_start", v1[:],
                             v_dram[KPAD * b + 128 * gp + 128:KPAD * b + 128 * gp + 192, :],
                             dma=True, deps=[fr["v"][par]])
                    bb3 = bb.rearrange("p (h c) -> p h c", c=192)
                    t_bbs = []
                    for m in range(4):
                        srcap = AP(g_dram.tensor, GROW * g + 2304 * m + 1,
                                   [[71, 32], [HSTR, H], [1, 71]])
                        t_bbs.append(E(
                            "gpsimd", "dma_start",
                            bb3[32 * m:32 * m + 32, :, 32 * m + 1:32 * m + 72],
                            srcap, dma=True,
                            deps=[fr["bb"][par], t_bbinit[par]]))
                    t_bb = t_bbs[-1]

                    AC_ORDER = [0, 4, 1, 5, 2, 6, 3, 7]
                    CH_ORDER = [0, 2, 1, 3]
                    SIDX = {}
                    for ci, c in enumerate(CH_ORDER):
                        for k2 in range(4):
                            SIDX[4 * c + k2] = 4 * ci + k2
                    t_log = [None] * 8
                    ac_last = None
                    for pos, i2 in enumerate(AC_ORDER):
                        ha, hb_ = QR_ORDER[2 * i2], QR_ORDER[2 * i2 + 1]
                        bk = pos % 4
                        base = 64 * (ha & 1)
                        deps0 = [t_q, t_k, pb_free[bk]]
                        m_a = E("tensor", "matmul", pb[bk][:, 0:192],
                                qg[base:base + 64, ha >> 1, :],
                                kg[base:base + 64, ha >> 1, :],
                                start=True, stop=True, deps=deps0)
                        m_b = E("tensor", "matmul", pb[bk][:, 192:384],
                                qg[base:base + 64, hb_ >> 1, :],
                                kg[base:base + 64, hb_ >> 1, :],
                                start=True, stop=True)
                        ac_last = m_b
                        ev = E("vector", "scalar_tensor_tensor",
                               lg[:, 384 * i2:384 * i2 + 384],
                               pb[bk][:, 0:384], 1.0 / SOFTCAP,
                               bb[:, 384 * i2:384 * i2 + 384],
                               ALU.mult, ALU.add,
                               deps=[m_b, fr["logt"][par]] + t_bbs)
                        pb_free[bk] = ev
                        t_log[i2] = ev
                    fr["qk"][par] = ac_last
                    fr["bb"][par] = t_log[AC_ORDER[-1]]
                    t_ex = [None] * H
                    t_rc = [None, None]
                    t_pr = [None] * H
                    last_pr = None
                    for ci, c in enumerate(CH_ORDER):
                        t_th = E("scalar", "activation",
                                 tv[:, 768 * c:768 * c + 768],
                                 lg[:, 768 * c:768 * c + 768], AF.Tanh,
                                 deps=[t_log[2 * c], t_log[2 * c + 1],
                                       fr["tv"][par]])
                        for k2 in range(4):
                            s = 4 * c + k2
                            t_ex[s] = E("scalar", "activation",
                                        lg[:, 192 * s:192 * s + 192],
                                        tv[:, 192 * s:192 * s + 192], AF.Exp,
                                        scale=SOFTCAP,
                                        accum_out=srow[:, 32 + SIDX[s]:33 + SIDX[s]],
                                        deps=[t_th, fr["srow"][par]])
                        if ci == 1 or ci == 3:
                            h0 = 8 * (ci // 2)
                            t_rc[ci // 2] = E(
                                "vector", "reciprocal",
                                srow[:, 48 + h0:56 + h0],
                                srow[:, 32 + h0:40 + h0],
                                deps=[t_ex[4 * CH_ORDER[ci - 1] + k2]
                                      for k2 in range(4)]
                                + [t_ex[4 * c + k2] for k2 in range(4)]
                                + [fr["srow"][par]])
                            for cc in (CH_ORDER[ci - 1], c):
                                for k2 in range(4):
                                    s = 4 * cc + k2
                                    t_pr[s] = E(
                                        "vector", "tensor_scalar_mul",
                                        prob[:, s, :],
                                        lg[:, 192 * s:192 * s + 192],
                                        srow[:, 48 + SIDX[s]:49 + SIDX[s]],
                                        deps=[t_rc[ci // 2], t_ex[s],
                                              fr["prob"][par]])
                                    last_pr = t_pr[s]
                    fr["logt"][par] = last_pr
                    fr["tv"][par] = t_ex[4 * CH_ORDER[-1] + 3]
                    fr["srow"][par] = last_pr
                    t_pT = [None] * 8
                    tr_last = None
                    for pos, s2_ in enumerate(AC_ORDER):
                        bk = 4 + (pos & 1)
                        mms = []
                        for j in range(2):
                            s = 2 * s2_ + j
                            mms.append(E("tensor", "matmul",
                                         pb[bk][:, 256 * j:256 * j + 128],
                                         prob[:, s, 0:128], ident[:],
                                         start=True, stop=True,
                                         deps=([t_pr[s], t_id, pb_free[bk]]
                                               if j == 0 else [t_pr[s]])))
                            mms.append(E("tensor", "matmul",
                                         pb[bk][0:64, 256 * j + 128:256 * j + 256],
                                         prob[:, s, 128:192], ident[:],
                                         start=True, stop=True))
                        tr_last = mms[-1]
                        ev = E("scalar", "activation", pT[:, 2 * s2_:2 * s2_ + 2, :],
                               pb[bk][:, 0:512], AF.Copy, deps=[mms[-1]])                             if (pos & 2) else                             E("vector", "tensor_copy", pT[:, 2 * s2_:2 * s2_ + 2, :],
                              pb[bk][:, 0:512], deps=[mms[-1]])
                        pb_free[bk] = ev
                        t_pT[s2_] = ev
                    fr["prob"][par] = tr_last
                    t_avev = []
                    av_last = None
                    for hp in range(8):
                        bk = 6 + (hp & 1)
                        mlast = None
                        for h01 in range(2):
                            h = 2 * hp + h01
                            s = QR_SLOT[h]
                            tp = (0, 64) if h01 else None
                            pev = t_pT[s >> 1]
                            E("tensor", "matmul",
                              pb[bk][64 * h01:64 * h01 + 64, 0:128],
                              v0[:, 64 * h:64 * h + 64], pT[:, s, 0:128],
                              start=True, stop=False,
                              deps=([t_v0, pev, pb_free[bk]] if h01 == 0
                                    else [t_v0, pev]),
                              inc=False, tile_position=tp)
                            mlast = E("tensor", "matmul",
                                      pb[bk][64 * h01:64 * h01 + 64, 0:128],
                                      v1[0:64, 64 * h:64 * h + 64],
                                      pT[0:64, s, 128:256],
                                      start=False, stop=True,
                                      deps=[t_v1], tile_position=tp)
                        av_last = mlast
                        ev = E("scalar", "activation", ast[:, hp, :],
                               pb[bk][:, 0:128], AF.Copy,
                               deps=[mlast, st2[par]]) if (hp & 2) else                             E("vector", "tensor_copy", ast[:, hp, :],
                              pb[bk][:, 0:128], deps=[mlast, st2[par]])
                        pb_free[bk] = ev
                        t_avev.append(ev)
                    fr["v"][par] = av_last
                    t_ast = E("scalar", "dma_start",
                              attn_dram[:, :, 128 * g:128 * g + 128], ast[:],
                              dma=True, deps=t_avev)
                    st2[par] = t_ast

        sch.barrier()

        def emit_ln_stage(wname, w_dram, in_dram, in_is_hp8, res_dram, scale_v,
                          out_f32_dram, out_is_ext, tag, prefetch=()):
            ICN = 8 if in_is_hp8 else 32
            with ExitStack() as sctx:
                w_bf = sctx.enter_context(
                    nc.sbuf_tensor(f"w_{tag}", [128, ICN, HID], BF16))
                ing = [sctx.enter_context(
                    nc.sbuf_tensor(f"in_{tag}{i}", [128, ICN, TT], BF16))
                    for i in range(2)]
                zt = [sctx.enter_context(
                    nc.sbuf_tensor(f"z_{tag}{i}", [128, 8, TT], F32))
                    for i in range(3)]
                zbf = sctx.enter_context(nc.sbuf_tensor(f"zb_{tag}", [128, 2, TT], BF16))
                zsq = sctx.enter_context(nc.sbuf_tensor(f"zq_{tag}", [128, 2, TT], BF16))
                mu_t = sctx.enter_context(nc.sbuf_tensor(f"mu_{tag}", [128, TT], F32))
                e2_t = sctx.enter_context(nc.sbuf_tensor(f"e2_{tag}", [128, TT], F32))
                m2_t = sctx.enter_context(nc.sbuf_tensor(f"m2_{tag}", [128, TT], F32))

                t_w = E("gpsimd", "dma_start", w_bf[:],
                        w_dram.rearrange("(ic p) o -> p ic o", p=128), dma=True)
                pf_toks = [E("gpsimd", "dma_start", dst, src, dma=True)
                           for dst, src in prefetch]
                stf = [None, None, None]
                mm_free = [None, None]
                pend = None
                prev_ms = [None, None]
                prev_mq = [None, None]
                stats_free = None
                for it in range(NT + 1):
                    newpend = None
                    if it < NT:
                        par = it & 1
                        j = it % 3
                        b, t0 = it // 4, TT * (it % 4)
                        tk0 = 2048 * b + t0
                        ig = ing[par]
                        t_in = E("sync", "dma_start", ig[:],
                                 in_dram[:, :, tk0:tk0 + TT], dma=True,
                                 deps=[mm_free[par]])
                        if res_dram is x_fm:
                            t_res = E("sync", "dma_start", zt[j][:],
                                      x_fm[b].rearrange("(ic p) t -> p ic t", p=128)
                                      [:, :, t0:t0 + TT], dma=True,
                                      deps=[stf[j]])
                        else:
                            t_res = E("sync", "dma_start", zt[j][:],
                                      res_dram[:, :, tk0:tk0 + TT], dma=True,
                                      deps=[stf[j]])
                        t_z = []
                        for oc in range(8):
                            bk = oc & 1
                            mt = None
                            for ic in range(ICN):
                                mt = E("tensor", "matmul", pb[bk][:, 0:TT],
                                       w_bf[:, ic, 128 * oc:128 * oc + 128],
                                       ig[:, ic, :],
                                       start=(ic == 0), stop=(ic == ICN - 1),
                                       deps=[t_in, t_w, pb_free[bk]] if ic == 0 else [],
                                       inc=(ic == ICN - 1))
                            if scale_v is None:
                                ev = E("vector", "tensor_add", zt[j][:, oc, :],
                                       pb[bk][:, 0:TT], zt[j][:, oc, :],
                                       deps=[mt, t_res])
                            else:
                                ev = E("vector", "scalar_tensor_tensor",
                                       zt[j][:, oc, :],
                                       pb[bk][:, 0:TT], scale_v[:, oc:oc + 1],
                                       zt[j][:, oc, :], ALU.mult, ALU.add,
                                       deps=[mt, t_res])
                            pb_free[bk] = ev
                            t_z.append(ev)
                        mm_free[par] = mt
                        newpend = (it, j, t_z)
                    if pend is not None:
                        it0, j0, t_z = pend
                        b0, t00 = it0 // 4, TT * (it0 % 4)
                        tk00 = 2048 * b0 + t00
                        bkS, bkQ = (2, 3) if (it0 & 1) == 0 else (4, 5)
                        m_s = None
                        m_q = None
                        for oc in range(8):
                            t_cb = E("scalar", "activation", zbf[:, oc & 1, :],
                                     zt[j0][:, oc, :], AF.Copy,
                                     deps=[t_z[oc], prev_ms[oc & 1]])
                            t_cq = E("scalar", "activation", zsq[:, oc & 1, :],
                                     zt[j0][:, oc, :],
                                     AF.Square, deps=[t_z[oc], prev_mq[oc & 1]])
                            m_s = E("tensor", "matmul", pb[bkS][:, 0:TT],
                                    ones128[:], zbf[:, oc & 1, :],
                                    start=(oc == 0), stop=(oc == 7),
                                    deps=[t_cb, t_o128] + (
                                        [pb_free[bkS]] if oc == 0 else []),
                                    inc=True)
                            m_q = E("tensor", "matmul", pb[bkQ][:, 0:TT],
                                    ones128[:], zsq[:, oc & 1, :],
                                    start=(oc == 0), stop=(oc == 7),
                                    deps=[t_cq, t_o128] + (
                                        [pb_free[bkQ]] if oc == 0 else []),
                                    inc=True)
                            prev_ms[oc & 1] = m_s
                            prev_mq[oc & 1] = m_q
                        t_mu = E("vector", "tensor_scalar_mul", mu_t[:],
                                 pb[bkS][:, 0:TT], 1.0 / HID,
                                 deps=[m_s, stats_free])
                        pb_free[bkS] = t_mu
                        t_e2 = E("vector", "tensor_scalar_mul", e2_t[:],
                                 pb[bkQ][:, 0:TT], 1.0 / HID,
                                 deps=[m_q, stats_free])
                        pb_free[bkQ] = t_e2
                        t_m2 = E("vector", "tensor_tensor", m2_t[:],
                                 mu_t[:], mu_t[:], ALU.mult, deps=[t_mu])
                        t_var = E("vector", "tensor_sub", e2_t[:],
                                  e2_t[:], m2_t[:], deps=[t_e2, t_m2])
                        t_sd = E("scalar", "activation", m2_t[:], e2_t[:],
                                 AF.Ln, bias=epsv[:, 0:1],
                                 deps=[t_var, t_eps])
                        t_rs = E("scalar", "activation", e2_t[:], m2_t[:],
                                 AF.Exp, scale=-0.5,
                                 deps=[t_sd])
                        t_mr = E("vector", "tensor_tensor", mu_t[:],
                                 mu_t[:], e2_t[:], ALU.mult, deps=[t_rs])
                        t_o = []
                        for oc in range(8):
                            t_t1 = E("vector", "tensor_mul", zt[j0][:, oc, :],
                                     zt[j0][:, oc, :], e2_t[:],
                                     deps=[t_rs, t_z[oc]])
                            t_t2 = E("vector", "tensor_sub", zt[j0][:, oc, :],
                                     zt[j0][:, oc, :], mu_t[:],
                                     deps=[t_mr, t_t1])
                            t_o.append(t_t2)
                        stats_free = t_o[-1]
                        if out_is_ext:
                            st = E("gpsimd", "dma_start",
                                   out_fm[b0].rearrange(
                                       "(oc p) t -> p oc t", p=128)
                                   [:, :, t00:t00 + TT],
                                   zt[j0][:], dma=True, deps=t_o)
                        else:
                            st = E("gpsimd", "dma_start",
                                   out_f32_dram[:, :, tk00:tk00 + TT],
                                   zt[j0][:], dma=True, deps=t_o)
                        stf[j0] = st
                    pend = newpend
                return pf_toks

        s4ctx = ExitStack()
        if upto >= 4:
            w1b = s4ctx.enter_context(nc.sbuf_tensor("w1b", [128, 8, DH], BF16))
        if upto >= 3:
            pf = emit_ln_stage(
                "post", wpostT, attn_dram, True, x_fm, None,
                x1f_dram, False, "p",
                prefetch=([(w1b[:], w1T.rearrange("(ic p) o -> p ic o", p=128))]
                          if upto >= 4 else ()))
        else:
            pf = []

        sch.barrier()

        if upto >= 4:
            with s4ctx as sctx:
                w2b = sctx.enter_context(nc.sbuf_tensor("w2b", [128, 8, DH], BF16))
                x1g = [sctx.enter_context(nc.sbuf_tensor(f"x1g{i}", [128, 8, TT], BF16))
                       for i in range(2)]
                sil = sctx.enter_context(nc.sbuf_tensor("sil", [128, TT], F32))
                ast = sctx.enter_context(nc.sbuf_tensor("a_st", [128, 32, TT], BF16))
                t_w1 = pf[0] if pf else E(
                    "gpsimd", "dma_start", w1b[:],
                    w1T.rearrange("(ic p) o -> p ic o", p=128), dma=True)
                t_w2 = E("gpsimd", "dma_start", w2b[:],
                         w2T.rearrange("(ic p) o -> p ic o", p=128), dma=True)
                stf = [None, None]
                mm_free = [None, None]
                st_half = [None, None]
                for it in range(NT):
                    par = it & 1
                    b, t0 = it // 4, TT * (it % 4)
                    tk0 = 2048 * b + t0
                    xg = x1g[par]
                    t_in = E("gpsimd", "dma_start", xg[:],
                             x1f_dram[:, :, tk0:tk0 + TT], dma=True,
                             deps=[mm_free[par]])
                    t_as = []
                    prev_stt = None
                    for oc in range(32):
                        bk0, bk1 = (0, 1) if (oc & 1) == 0 else (2, 3)
                        m1 = None
                        for ic in range(8):
                            m1 = E("tensor", "matmul", pb[bk0][:, 0:TT],
                                   w1b[:, ic, 128 * oc:128 * oc + 128], xg[:, ic, :],
                                   start=(ic == 0), stop=(ic == 7),
                                   deps=[t_in, t_w1, pb_free[bk0]] if ic == 0 else [],
                                   inc=(ic == 7))
                        t_si = E("scalar", "activation", sil[:], pb[bk0][:, 0:TT],
                                 AF.Silu, scale=s1v[:, oc:oc + 1],
                                 deps=[m1, t_s1, prev_stt])
                        pb_free[bk0] = t_si
                        m2 = None
                        for ic in range(8):
                            m2 = E("tensor", "matmul", pb[bk1][:, 0:TT],
                                   w2b[:, ic, 128 * oc:128 * oc + 128], xg[:, ic, :],
                                   start=(ic == 0), stop=(ic == 7),
                                   deps=[t_in, t_w2, pb_free[bk1]] if ic == 0 else [],
                                   inc=(ic == 7))
                        ev = E("vector", "scalar_tensor_tensor", ast[:, oc, :],
                               pb[bk1][:, 0:TT], s2v[:, oc:oc + 1], sil[:],
                               ALU.mult, ALU.mult,
                               deps=[m2, t_si, t_s2, st_half[oc // 16]],
                               )
                        pb_free[bk1] = ev
                        prev_stt = ev
                        t_as.append(ev)
                        if oc == 15:
                            st_half[0] = E("scalar", "dma_start",
                                           a_dram[:, 0:16, tk0:tk0 + TT],
                                           ast[:, 0:16, :], dma=True,
                                           deps=t_as[0:16])
                    st_half[1] = E("scalar", "dma_start",
                                   a_dram[:, 16:32, tk0:tk0 + TT],
                                   ast[:, 16:32, :], dma=True, deps=t_as[16:32])
                    stf[par] = st_half[1]
                    mm_free[par] = m2

        sch.barrier()

        HPW = 256
        HNP = DH // HPW
        HST = 4
        HSUP = NT // HST

        def emit_h_stage(hT, hs_v, in_dram, out_dram, tag, bufs,
                         t_pn0=None, next_hT=None):
            panel, slab, bst = bufs
            hT_r = hT.rearrange("(ic p) o -> p ic o", p=128)
            stf = [None, None]
            pn_free = [None, None]
            t_pn = [None, None]
            t_pn[0] = t_pn0 if t_pn0 is not None else E(
                "gpsimd", "dma_start", panel[0][:], hT_r[:, :, 0:HPW],
                dma=True)
            t_sl = [None] * HST
            for j in range(HST):
                t_sl[j] = E("sync", "dma_start", slab[j][:],
                            in_dram[:, :, TT * j:TT * j + TT], dma=True)
            ret = None
            for sup in range(HSUP):
                for P in range(HNP):
                    pp = P & 1
                    last = (sup == HSUP - 1) and (P == HNP - 1)
                    if not last:
                        Pn = (P + 1) % HNP
                        t_pn[1 - pp] = E("gpsimd", "dma_start",
                                         panel[1 - pp][:],
                                         hT_r[:, :, HPW * Pn:HPW * Pn + HPW],
                                         dma=True, deps=[pn_free[1 - pp]])
                    elif next_hT is not None:
                        nhT_r = next_hT.rearrange("(ic p) o -> p ic o", p=128)
                        ret = E("gpsimd", "dma_start", panel[0][:],
                                nhT_r[:, :, 0:HPW], dma=True,
                                deps=[pn_free[0]])
                    last_mm = None
                    for j in range(HST):
                        it = HST * sup + j
                        tk0 = TT * it
                        evs = []
                        for oc in range(2):
                            bk = (2 * j + oc) & 7
                            mt = None
                            for ic in range(32):
                                mt = E("tensor", "matmul", pb[bk][:, 0:TT],
                                       panel[pp][:, ic, 128 * oc:128 * oc + 128],
                                       slab[j][:, ic, :],
                                       start=(ic == 0), stop=(ic == 31),
                                       deps=([t_sl[j], t_pn[pp], pb_free[bk]]
                                             if ic == 0 else []),
                                       inc=(ic == 31))
                            ev = E("scalar", "activation",
                                   bst[j & 1][:, oc, :],
                                   pb[bk][:, 0:TT], AF.Gelu,
                                   scale=hs_v[:, 2 * P + oc:2 * P + oc + 1],
                                   deps=[mt, stf[j & 1]])
                            pb_free[bk] = ev
                            evs.append(ev)
                            last_mm = mt
                        stf[j & 1] = E("scalar", "dma_start",
                                       out_dram[:, 2 * P:2 * P + 2,
                                                tk0:tk0 + TT],
                                       bst[j & 1][:], dma=True, deps=evs)
                        if P == HNP - 1 and sup + 1 < HSUP:
                            it2 = HST * (sup + 1) + j
                            t_sl[j] = E("sync", "dma_start", slab[j][:],
                                        in_dram[:, :, TT * it2:TT * it2 + TT],
                                        dma=True, deps=[last_mm])
                    pn_free[pp] = last_mm
            return ret

        if upto >= 5 or upto >= 6:
            with ExitStack() as hctx:
                hbufs = (
                    [hctx.enter_context(
                        nc.sbuf_tensor(f"pn_h{i}", [128, 32, HPW], BF16))
                     for i in range(2)],
                    [hctx.enter_context(
                        nc.sbuf_tensor(f"sl_h{j}", [128, 32, TT], BF16))
                     for j in range(HST)],
                    [hctx.enter_context(
                        nc.sbuf_tensor(f"bs_h{i}", [128, 2, TT], BF16))
                     for i in range(2)],
                )
                t_pn2 = None
                if upto >= 5:
                    t_pn2 = emit_h_stage(h1T, h1v, a_dram, b_dram, "h1", hbufs,
                                         next_hT=(h2T if upto >= 6 else None))
                sch.barrier()
                if upto >= 6:
                    emit_h_stage(h2T, h2v, b_dram, c_dram, "h2", hbufs,
                                 t_pn0=t_pn2)
        sch.barrier()

        if upto >= 7:
            emit_ln_stage("w3", w3T, c_dram, False, x1f_dram, w3v,
                          None, True, "y")

        sch.flush()
        sch.final_wait()

    return nc



def kernel(L0_attn_hidden_states, L0_attn_position_embeddings, L0_attn_w_q,
           L0_attn_w_k, L0_attn_w_v, L0_attn_w_rel, L0_attn_w_post,
           L0_attn_per_dim_scale, L0_ffn_w1_w, L0_ffn_w1_s, L0_ffn_w2_w,
           L0_ffn_w2_s, L0_ffn_h1_w, L0_ffn_h1_s, L0_ffn_h2_w, L0_ffn_h2_s,
           L0_ffn_w3_w, L0_ffn_w3_s, _upto=99, _dump=(), _nc=None):
    f32 = np.float32
    x = np.asarray(L0_attn_hidden_states, f32)
    B = x.shape[0]
    n_cores = 8
    bpc = B // n_cores

    def T(a):
        return np.ascontiguousarray(np.asarray(a, f32).T)

    common = {
        "wqT": T(L0_attn_w_q), "wkT": T(L0_attn_w_k), "wvT": T(L0_attn_w_v),
        "wrelT": T(L0_attn_w_rel), "wpostT": T(L0_attn_w_post),
        "posT": T(L0_attn_position_embeddings),
        "pds": np.ascontiguousarray(np.asarray(L0_attn_per_dim_scale, f32)),
        "w1T": T(L0_ffn_w1_w), "w2T": T(L0_ffn_w2_w),
        "h1T": T(L0_ffn_h1_w), "h2T": T(L0_ffn_h2_w), "w3T": T(L0_ffn_w3_w),
        "s1": np.ascontiguousarray(np.asarray(L0_ffn_w1_s, f32).reshape(32, 128).T),
        "s2": np.ascontiguousarray(np.asarray(L0_ffn_w2_s, f32).reshape(32, 128).T),
        "h1s": np.ascontiguousarray(np.asarray(L0_ffn_h1_s, f32).reshape(32, 128).T),
        "h2s": np.ascontiguousarray(np.asarray(L0_ffn_h2_s, f32).reshape(32, 128).T),
        "w3s": np.ascontiguousarray(np.asarray(L0_ffn_w3_s, f32).reshape(8, 128).T),
    }
    x_fm = np.ascontiguousarray(x.transpose(0, 2, 1))
    in_maps = []
    for c in range(n_cores):
        m = dict(common)
        m["x_fm"] = np.ascontiguousarray(x_fm[c * bpc:(c + 1) * bpc])
        in_maps.append(m)

    nc = _nc if _nc is not None else build_program(upto=_upto, dump=_dump)
    res = run_bass_kernel_spmd(nc, in_maps, core_ids=list(range(n_cores)))
    if _dump:
        return res
    out = np.concatenate([r["out_fm"] for r in res.results], axis=0)
    return np.ascontiguousarray(out.transpose(0, 2, 1))


# version 13
# speedup vs baseline: 1.0747x; 1.0185x over previous
import math
import os
from contextlib import ExitStack
S1_MODE = os.environ.get("K_S1_MODE", "full")
S1_NT = int(os.environ.get("K_S1_NT", "999"))
S1_QRH = int(os.environ.get("K_S1_QRH", "16"))

import numpy as np

import concourse.bass as bass
import concourse.mybir as mybir
from concourse.bass_utils import run_bass_kernel_spmd
from concourse.ap import AP

F32 = mybir.dt.float32
BF16 = mybir.dt.bfloat16
AF = mybir.ActivationFunctionType
ALU = mybir.AluOpType
AX = mybir.AxisListType

H = 16
D = 64
HID = 1024
DH = 4096
CHUNK = 32
PAST, FUT = 31, 8
CTX = 71
SEQ = 2048
NSEQ = 4
NTOK = NSEQ * SEQ
TT = 512
NT = NTOK // TT
NGS = SEQ // 128
NG = NSEQ * NGS
KPAD = SEQ + 64
GROW = 72 * 128
HSTR = 1 + NTOK * 72 + 7

QR_ORDER = [0, 2, 4, 6, 8, 10, 12, 14, 1, 3, 5, 7, 9, 11, 13, 15]
QR_SLOT = {h: i for i, h in enumerate(QR_ORDER)}

Q_SCALE = D ** (-0.5) / math.log(2)
K_SCALE = math.log(1.0 + math.e) / math.log(2)
SOFTCAP = 50.0

STRICT_SAME_ENGINE = True

ENGINES = ("tensor", "vector", "scalar", "gpsimd", "sync")
DMA_QUEUES = ("gpsimd", "sync", "scalar")
NDMA_SLOTS = 4


class Sched:

    def __init__(self, nc, ctx, block):
        self.nc = nc
        self.block = block
        names = list(ENGINES)
        for q in DMA_QUEUES:
            for s in range(NDMA_SLOTS):
                names.append(f"dq_{q}{s}")
        self.sems = {n: ctx.enter_context(nc.semaphore(f"s_{n}")) for n in names}
        self.count = {n: 0 for n in names}
        self.queues = {e: [] for e in ENGINES}
        self.dma_uses = {q: 0 for q in DMA_QUEUES}
        self.last_wait = {e: {n: 0 for n in names} for e in ENGINES}
        self.pending_barrier = {e: {} for e in ENGINES}

    def emit(self, eng, fn_name, *args, deps=(), inc=True, dma=False, **kwargs):
        if dma:
            slot = self.dma_uses[eng] % NDMA_SLOTS
            self.dma_uses[eng] += 1
            sem_name = f"dq_{eng}{slot}"
        else:
            sem_name = eng
        waits = dict(self.pending_barrier[eng])
        self.pending_barrier[eng] = {}
        if dma and self.count[sem_name] > 0:
            if self.count[sem_name] > waits.get(sem_name, 0):
                waits[sem_name] = self.count[sem_name]
        for tok in deps:
            if tok is None:
                continue
            n, v = tok
            if n == eng and not STRICT_SAME_ENGINE:
                continue
            if v > waits.get(n, 0):
                waits[n] = v
        real_waits = []
        for n, v in waits.items():
            if v > self.last_wait[eng][n]:
                real_waits.append((n, v))
                self.last_wait[eng][n] = v
        tok = None
        if inc:
            self.count[sem_name] += 16 if dma else 1
            tok = (sem_name, self.count[sem_name])

        def step(engobj, fn_name=fn_name, args=args, kwargs=kwargs,
                 real_waits=real_waits, inc=inc, dma=dma, sem_name=sem_name):
            for n, v in real_waits:
                engobj.wait_ge(self.sems[n], v)
            inst = getattr(engobj, fn_name)(*args, **kwargs)
            if inc:
                inst.then_inc(self.sems[sem_name], 16 if dma else 1)

        self.queues[eng].append(step)
        return tok

    def raw(self, eng, fn):
        self.queues[eng].append(fn)

    def barrier(self):
        snap = {n: v for n, v in self.count.items() if v > 0}
        for e in ENGINES:
            merged = dict(self.pending_barrier[e])
            for n, v in snap.items():
                if v > merged.get(n, 0):
                    merged[n] = v
            self.pending_barrier[e] = merged

    def flush(self):
        block = self.block
        for e in ENGINES:
            q = self.queues[e]
            if not q:
                continue
            self.queues[e] = []
            section = getattr(block, e)

            def body(engobj, q=q):
                for step in q:
                    step(engobj)

            section(body)

    def final_wait(self):
        snap = {n: v for n, v in self.count.items() if v > 0}

        def body(engobj):
            for n, v in snap.items():
                engobj.wait_ge(self.sems[n], v)

        self.block.sync(body)


def build_program(upto=99, dump=()):
    nc = bass.Bass("TRN2", target_bir_lowering=False, debug=False)

    def dram(name, shape, dtype, kind="Internal"):
        k = "ExternalOutput" if name in dump else kind
        return nc.dram_tensor(name, shape, dtype, kind=k).ap()

    x_fm = nc.dram_tensor("x_fm", [NSEQ, HID, SEQ], F32, kind="ExternalInput").ap()
    wqT = nc.dram_tensor("wqT", [HID, HID], F32, kind="ExternalInput").ap()
    wkT = nc.dram_tensor("wkT", [HID, HID], F32, kind="ExternalInput").ap()
    wvT = nc.dram_tensor("wvT", [HID, HID], F32, kind="ExternalInput").ap()
    wrelT = nc.dram_tensor("wrelT", [HID, HID], F32, kind="ExternalInput").ap()
    wpostT = nc.dram_tensor("wpostT", [HID, HID], F32, kind="ExternalInput").ap()
    posT = nc.dram_tensor("posT", [HID, CTX], F32, kind="ExternalInput").ap()
    pdsd = nc.dram_tensor("pds", [D], F32, kind="ExternalInput").ap()
    w1T = nc.dram_tensor("w1T", [HID, DH], F32, kind="ExternalInput").ap()
    w2T = nc.dram_tensor("w2T", [HID, DH], F32, kind="ExternalInput").ap()
    h1T = nc.dram_tensor("h1T", [DH, DH], F32, kind="ExternalInput").ap()
    h2T = nc.dram_tensor("h2T", [DH, DH], F32, kind="ExternalInput").ap()
    w3T = nc.dram_tensor("w3T", [DH, HID], F32, kind="ExternalInput").ap()
    s1d = nc.dram_tensor("s1", [128, 32], F32, kind="ExternalInput").ap()
    s2d = nc.dram_tensor("s2", [128, 32], F32, kind="ExternalInput").ap()
    h1sd = nc.dram_tensor("h1s", [128, 32], F32, kind="ExternalInput").ap()
    h2sd = nc.dram_tensor("h2s", [128, 32], F32, kind="ExternalInput").ap()
    w3sd = nc.dram_tensor("w3s", [128, 8], F32, kind="ExternalInput").ap()

    out_fm = nc.dram_tensor("out_fm", [NSEQ, HID, SEQ], F32,
                            kind="ExternalOutput").ap()

    q_dram = dram("q_dram", [128, 8, NTOK], BF16)
    k_dram = dram("k_dram", [128, 8, NSEQ * KPAD], BF16)
    v_dram = dram("v_dram", [NSEQ * KPAD, HID], BF16)
    g_dram = dram("g_dram", [H, HSTR], F32)
    attn_dram = dram("attn_dram", [128, 8, NTOK], BF16)
    x1f_dram = dram("x1f_dram", [128, 8, NTOK], F32)
    a_dram = dram("a_dram", [128, 32, NTOK], BF16)
    b_dram = dram("b_dram", [128, 32, NTOK], BF16)
    c_dram = dram("c_dram", [128, 32, NTOK], BF16)

    ctx0 = ExitStack()
    with ctx0:
        block = ctx0.enter_context(nc.Block())
        sch = Sched(nc, ctx0, block)

        ident = ctx0.enter_context(nc.sbuf_tensor("ident", [128, 128], BF16))
        sq128 = ctx0.enter_context(nc.sbuf_tensor("sq128", [128, 8], F32))
        relk = ctx0.enter_context(nc.sbuf_tensor("relk", [128, 8, 72], BF16))
        ones_bf = ctx0.enter_context(nc.sbuf_tensor("ones_bf", [128, 2], BF16))
        ones128 = ctx0.enter_context(nc.sbuf_tensor("ones128", [128, 128], BF16))
        ones_f1 = ctx0.enter_context(nc.sbuf_tensor("ones_f1", [1, 128], F32))
        pb = [ctx0.enter_context(nc.psum_tensor(f"pb{i}", [128, 512], F32))
              for i in range(8)]
        pb_free = {i: None for i in range(8)}

        s1v = ctx0.enter_context(nc.sbuf_tensor("s1v", [128, 32], F32))
        s2v = ctx0.enter_context(nc.sbuf_tensor("s2v", [128, 32], F32))
        h1v = ctx0.enter_context(nc.sbuf_tensor("h1v", [128, 32], F32))
        h2v = ctx0.enter_context(nc.sbuf_tensor("h2v", [128, 32], F32))
        w3v = ctx0.enter_context(nc.sbuf_tensor("w3v", [128, 8], F32))
        zeros = ctx0.enter_context(nc.sbuf_tensor("zeros", [128, 1024], BF16))
        zerof = ctx0.enter_context(nc.sbuf_tensor("zerof", [1, 16], F32))
        epsv = ctx0.enter_context(nc.sbuf_tensor("epsv", [128, 1], F32))

        E = sch.emit

        for i in range(8):
            pb_free[i] = E("vector", "memset", pb[i][:], 0.0)
        t_id0 = E("gpsimd", "memset", ident[:], 0.0)
        t_id = E("gpsimd", "affine_select", out=ident[:], in_=ident[:],
                 compare_op=ALU.not_equal, fill=1.0, base=0,
                 pattern=[[-1, 128]], channel_multiplier=1, deps=[t_id0])
        t_ones = E("gpsimd", "memset", ones_bf[:], 1.0)
        t_o128 = E("gpsimd", "memset", ones128[:], 1.0)
        t_ones1 = E("gpsimd", "memset", ones_f1[:], 1.0)
        t_zero = E("gpsimd", "memset", zeros[:], 0.0)
        t_zerof = E("gpsimd", "memset", zerof[:], 0.0)
        t_eps = E("gpsimd", "memset", epsv[:], 1e-5)

        t_s1 = E("sync", "dma_start", s1v[:], s1d, dma=True)
        t_s2 = E("sync", "dma_start", s2v[:], s2d, dma=True)
        t_h1 = E("sync", "dma_start", h1v[:], h1sd, dma=True)
        t_h2 = E("sync", "dma_start", h2v[:], h2sd, dma=True)
        t_w3 = E("sync", "dma_start", w3v[:], w3sd, dma=True)

        pds_ap = pdsd.rearrange("(p o) -> p o", o=1)
        t_p0 = E("sync", "dma_start", sq128[0:64, 1:2], pds_ap, dma=True)
        t_p1 = E("sync", "dma_start", sq128[64:128, 1:2], pds_ap, dma=True)
        t_a = E("scalar", "activation", sq128[:, 2:3], sq128[:, 1:2], AF.Abs,
                deps=[t_p0, t_p1])
        t_e = E("scalar", "activation", sq128[:, 3:4], sq128[:, 2:3], AF.Exp,
                scale=-1.0, deps=[t_a])
        t_l = E("scalar", "activation", sq128[:, 4:5], sq128[:, 3:4], AF.Ln,
                bias=1.0, deps=[t_e])
        t_m = E("vector", "tensor_scalar_max", sq128[:, 5:6], sq128[:, 1:2],
                0.0, deps=[t_p0, t_p1])
        t_sp = E("vector", "tensor_add", sq128[:, 6:7], sq128[:, 5:6],
                 sq128[:, 4:5], deps=[t_m, t_l])
        t_sq = E("vector", "tensor_scalar_mul", sq128[:, 0:1], sq128[:, 6:7],
                 Q_SCALE, deps=[t_sp])

        pad_toks = []
        for b in range(NSEQ):
            for off in (0, KPAD - 32):
                pad_toks.append(E("sync", "dma_start",
                                  k_dram[:, :, b * KPAD + off:b * KPAD + off + 32],
                                  zeros[:, 0:256].rearrange("p (a w) -> p a w", a=8),
                                  dma=True, deps=[t_zero]))
                pad_toks.append(E("sync", "dma_start",
                                  v_dram[b * KPAD + off:b * KPAD + off + 32, :],
                                  zeros[0:32, :], dma=True, deps=[t_zero]))
        t_guard = E("sync", "dma_start", g_dram[:, 0:1], zerof[0:1, :],
                    dma=True, deps=[t_zerof], allow_slow_non_contiguous=True)

        if upto >= 1:
            with ExitStack() as sctx:
                wq_bf = sctx.enter_context(nc.sbuf_tensor("wq_bf", [128, 8, HID], BF16))
                wk_bf = sctx.enter_context(nc.sbuf_tensor("wk_bf", [128, 8, HID], BF16))
                wv_bf = sctx.enter_context(nc.sbuf_tensor("wv_bf", [128, 8, HID], BF16))
                wr_bf = sctx.enter_context(nc.sbuf_tensor("wr_bf", [128, 8, HID], BF16))
                posb = sctx.enter_context(nc.sbuf_tensor("posb", [128, 8, 72], BF16))
                xbf = [sctx.enter_context(nc.sbuf_tensor(f"xbf{i}", [128, 8, TT], BF16))
                       for i in range(2)]
                qst = [sctx.enter_context(nc.sbuf_tensor(f"qst{i}", [128, 8, TT], BF16))
                       for i in range(2)]
                kst = [sctx.enter_context(nc.sbuf_tensor(f"kst{i}", [128, 8, TT], BF16))
                       for i in range(2)]
                vst = [sctx.enter_context(nc.sbuf_tensor(f"vst{i}", [128, 4, HID], BF16))
                       for i in range(2)]
                gst = [sctx.enter_context(nc.sbuf_tensor(f"gst{i}", [128, 4, H, 72], F32))
                       for i in range(2)]

                def wload(dst, src):
                    return E("gpsimd", "dma_start", dst[:],
                             src.rearrange("(ic p) o -> p ic o", p=128), dma=True)

                t_wq = wload(wq_bf, wqT)
                t_wk = wload(wk_bf, wkT)
                t_wv = wload(wv_bf, wvT)
                t_wr = wload(wr_bf, wrelT)
                t_posz = E("gpsimd", "memset", posb[:], 0.0)
                t_pos = E("gpsimd", "dma_start", posb[:, :, 0:71],
                          posT.rearrange("(ic p) l -> p ic l", p=128),
                          dma=True, deps=[t_posz])

                t_relk = []
                for oc in range(8 if S1_MODE != "w" else 0):
                    bk = 6 + (oc & 1)
                    mt = None
                    for ic in range(8):
                        mt = E("tensor", "matmul", pb[bk][:, 0:72],
                               wr_bf[:, ic, 128 * oc:128 * oc + 128],
                               posb[:, ic, :],
                               start=(ic == 0), stop=(ic == 7),
                               deps=[t_wr, t_pos, pb_free[bk]] if ic == 0 else [],
                               inc=(ic == 7))
                    ev = E("vector", "tensor_copy", relk[:, oc, :], pb[bk][:, 0:72],
                           deps=[mt])
                    pb_free[bk] = ev
                    t_relk.append(ev)

                qkv_done = [[None] * 3 for _ in range(NT)]
                g_done = [None] * NT
                st_free = [None, None]
                for it in range(min(NT, S1_NT) if S1_MODE not in ("w", "relk") else 0):
                    par = it & 1
                    b, t0 = it // 4, TT * (it % 4)
                    xb, qs, ks, vs, gs = xbf[par], qst[par], kst[par], vst[par], gst[par]
                    free_dep = st_free[par] or []
                    t_x = E("gpsimd", "dma_start", xb[:],
                            x_fm[b].rearrange("(ic p) t -> p ic t", p=128)
                            [:, :, t0:t0 + TT],
                            dma=True, deps=[t_wq] + list(free_dep))
                    ev_q = []
                    ev_k = []
                    for proj, wbf, stg, evl in (
                            ("q", wq_bf, qs, ev_q), ("k", wk_bf, ks, ev_k))[
                                :0 if S1_MODE == "x" else 2]:
                        for oc in range(8):
                            bk = oc & 1
                            mt = None
                            for ic in range(8):
                                mt = E("tensor", "matmul", pb[bk][:, 0:TT],
                                       wbf[:, ic, 128 * oc:128 * oc + 128],
                                       xb[:, ic, :],
                                       start=(ic == 0), stop=(ic == 7),
                                       deps=[t_x, pb_free[bk]] if ic == 0 else [],
                                       inc=(ic == 7))
                            if proj == "q":
                                ev = E("scalar", "activation", stg[:, oc, :],
                                       pb[bk][:, 0:TT], AF.Copy,
                                       scale=sq128[:, 0:1], deps=[mt, t_sq])
                            else:
                                ev = E("scalar", "activation", stg[:, oc, :],
                                       pb[bk][:, 0:TT], AF.Copy,
                                       scale=K_SCALE, deps=[mt])
                            pb_free[bk] = ev
                            evl.append(ev)
                    ev_v = []
                    for sub in range(4 if S1_MODE in ("full", "v", "qr") else 0):
                        for nh in range(2):
                            bk = 2 + nh
                            mt = None
                            for ic in range(8):
                                mt = E("tensor", "matmul", pb[bk][:, 0:512],
                                       xb[:, ic, 128 * sub:128 * sub + 128],
                                       wv_bf[:, ic, 512 * nh:512 * nh + 512],
                                       start=(ic == 0), stop=(ic == 7),
                                       deps=[t_x, t_wv, pb_free[bk]] if ic == 0 else [],
                                       inc=(ic == 7))
                            ev = E("scalar", "activation",
                                   vs[:, sub, 512 * nh:512 * nh + 512],
                                   pb[bk][:, 0:512], AF.Copy, deps=[mt])
                            pb_free[bk] = ev
                            ev_v.append(ev)
                    ev_g = []
                    for sub in range(4 if S1_MODE in ("full", "qr", "qr_nostore", "qr_noevac") else 0):
                        n_qr = min(16, S1_QRH)
                        for q4 in range(4):
                            heads = [h for h in QR_ORDER[4 * q4:4 * q4 + 4]
                                     if h < n_qr or S1_QRH >= 16]
                            if not heads:
                                continue
                            bk = 4 + q4
                            mts = []
                            for j, h in enumerate(heads):
                                mts.append(E(
                                    "tensor", "matmul",
                                    pb[bk][:, 128 * j:128 * j + 72],
                                    qs[64 * (h & 1):64 * (h & 1) + 64,
                                       h >> 1, 128 * sub:128 * sub + 128],
                                    relk[64 * (h & 1):64 * (h & 1) + 64, h >> 1, :],
                                    start=True, stop=True,
                                    deps=([ev_q[h >> 1], t_relk[h >> 1],
                                           pb_free[bk]] if j == 0 else
                                          [ev_q[h >> 1], t_relk[h >> 1]]),
                                    inc=(j == len(heads) - 1)))
                            if S1_MODE == "qr_noevac":
                                pb_free[bk] = mts[-1]
                                continue
                            ev = E("scalar", "activation",
                                   gs[:, sub, 4 * q4:4 * q4 + len(heads), :],
                                   pb[bk][:, 0:512].rearrange(
                                       "p (a w) -> p a w", a=4)[:, 0:len(heads), 0:72],
                                   AF.Copy,
                                   scale=1.0 / SOFTCAP, deps=[mts[-1]])
                            pb_free[bk] = ev
                            ev_g.append((sub, ev))
                    all_st = []
                    if ev_q:
                        tq = E("sync", "dma_start",
                               q_dram[:, :, 2048 * b + t0:2048 * b + t0 + TT],
                               qs[:], dma=True, deps=ev_q)
                        tk = E("sync", "dma_start",
                               k_dram[:, :, KPAD * b + 32 + t0:KPAD * b + 32 + t0 + TT],
                               ks[:], dma=True, deps=ev_k)
                        all_st += [tq, tk]
                    tvs = []
                    for sub in range(4 if ev_v else 0):
                        r0 = KPAD * b + 32 + t0 + 128 * sub
                        tvs.append(E("sync", "dma_start", v_dram[r0:r0 + 128, :],
                                     vs[:, sub, :], dma=True, deps=[ev_v[2 * sub],
                                                                    ev_v[2 * sub + 1]]))
                    all_st += tvs
                    tgs = []
                    for sub in range(4 if (ev_g and S1_MODE != "qr_nostore") else 0):
                        gi = 4 * it + sub
                        dst = AP(g_dram.tensor, 1 + GROW * gi,
                                 [[72, 128], [HSTR, H], [1, 72]])
                        tgs.append(E("sync", "dma_start", dst, gs[:, sub],
                                     dma=True,
                                     deps=[ev for s, ev in ev_g if s == sub]))
                    all_st += tgs
                    qkv_done[it] = [None, None, tvs]
                    g_done[it] = tgs
                    st_free[par] = all_st

        sch.barrier()

        if upto >= 2:
            with ExitStack() as sctx:
                q_g = [sctx.enter_context(nc.sbuf_tensor(f"q_g{i}", [128, 8, 128], BF16))
                       for i in range(3)]
                k_g = [sctx.enter_context(nc.sbuf_tensor(f"k_g{i}", [128, 8, 192], BF16))
                       for i in range(3)]
                v_g0 = [sctx.enter_context(nc.sbuf_tensor(f"v_g0{i}", [128, HID], BF16))
                        for i in range(3)]
                v_g1 = [sctx.enter_context(nc.sbuf_tensor(f"v_g1{i}", [64, HID], BF16))
                        for i in range(3)]
                bbt = [sctx.enter_context(nc.sbuf_tensor(f"bb{i}", [128, H * 192], F32))
                       for i in range(3)]
                logt = [sctx.enter_context(nc.sbuf_tensor(f"logt{i}", [128, H * 192], F32))
                        for i in range(3)]
                tvt = [sctx.enter_context(nc.sbuf_tensor(f"tvt{i}", [128, H * 192], F32))
                       for i in range(3)]
                probt = [sctx.enter_context(nc.sbuf_tensor(f"prob{i}", [128, H, 192], BF16))
                         for i in range(3)]
                pTt = [sctx.enter_context(nc.sbuf_tensor(f"pT{i}", [128, H, 256], BF16))
                       for i in range(3)]
                srowt = [sctx.enter_context(nc.sbuf_tensor(f"srow{i}", [128, 64], F32))
                         for i in range(3)]
                atst = [sctx.enter_context(nc.sbuf_tensor(f"atst{i}", [128, 8, 128], BF16))
                        for i in range(3)]
                fr = {k2: [None, None, None] for k2 in
                      ("qk", "v", "bb", "logt", "tv", "srow", "prob", "ast")}
                st2 = [None, None, None]
                t_bbinit = [E("gpsimd", "memset", bbt[i][:], -1.0e9)
                            for i in range(3)]
                for g in range(NG):
                    par = g % 3
                    b, gp = g // NGS, g % NGS
                    qg, kg, v0, v1 = q_g[par], k_g[par], v_g0[par], v_g1[par]
                    bb, lg, tv, prob = bbt[par], logt[par], tvt[par], probt[par]
                    pT, srow, ast = pTt[par], srowt[par], atst[par]
                    t_q = E("sync", "dma_start", qg[:],
                            q_dram[:, :, 128 * g:128 * g + 128], dma=True,
                            deps=[fr["qk"][par]])
                    t_k = E("sync", "dma_start", kg[:],
                            k_dram[:, :, KPAD * b + 128 * gp:KPAD * b + 128 * gp + 192],
                            dma=True, deps=[fr["qk"][par]])
                    t_v0 = E("sync", "dma_start", v0[:],
                             v_dram[KPAD * b + 128 * gp:KPAD * b + 128 * gp + 128, :],
                             dma=True, deps=[fr["v"][par]])
                    t_v1 = E("sync", "dma_start", v1[:],
                             v_dram[KPAD * b + 128 * gp + 128:KPAD * b + 128 * gp + 192, :],
                             dma=True, deps=[fr["v"][par]])
                    bb3 = bb.rearrange("p (h c) -> p h c", c=192)
                    t_bbs = []
                    for m in range(4):
                        srcap = AP(g_dram.tensor, GROW * g + 2304 * m + 1,
                                   [[71, 32], [HSTR, H], [1, 71]])
                        t_bbs.append(E(
                            "gpsimd", "dma_start",
                            bb3[32 * m:32 * m + 32, :, 32 * m + 1:32 * m + 72],
                            srcap, dma=True,
                            deps=[fr["bb"][par], t_bbinit[par]]))
                    t_bb = t_bbs[-1]

                    AC_ORDER = [0, 4, 1, 5, 2, 6, 3, 7]
                    CH_ORDER = [0, 2, 1, 3]
                    SIDX = {}
                    for ci, c in enumerate(CH_ORDER):
                        for k2 in range(4):
                            SIDX[4 * c + k2] = 4 * ci + k2
                    t_log = [None] * 8
                    ac_last = None
                    for pos, i2 in enumerate(AC_ORDER):
                        ha, hb_ = QR_ORDER[2 * i2], QR_ORDER[2 * i2 + 1]
                        bk = pos % 4
                        base = 64 * (ha & 1)
                        deps0 = [t_q, t_k, pb_free[bk]]
                        m_a = E("tensor", "matmul", pb[bk][:, 0:192],
                                qg[base:base + 64, ha >> 1, :],
                                kg[base:base + 64, ha >> 1, :],
                                start=True, stop=True, deps=deps0)
                        m_b = E("tensor", "matmul", pb[bk][:, 192:384],
                                qg[base:base + 64, hb_ >> 1, :],
                                kg[base:base + 64, hb_ >> 1, :],
                                start=True, stop=True)
                        ac_last = m_b
                        ev = E("vector", "scalar_tensor_tensor",
                               lg[:, 384 * i2:384 * i2 + 384],
                               pb[bk][:, 0:384], 1.0 / SOFTCAP,
                               bb[:, 384 * i2:384 * i2 + 384],
                               ALU.mult, ALU.add,
                               deps=[m_b, fr["logt"][par]] + t_bbs)
                        pb_free[bk] = ev
                        t_log[i2] = ev
                    fr["qk"][par] = ac_last
                    fr["bb"][par] = t_log[AC_ORDER[-1]]
                    t_ex = [None] * H
                    t_rc = [None, None]
                    t_pr = [None] * H
                    last_pr = None
                    for ci, c in enumerate(CH_ORDER):
                        t_th = E("scalar", "activation",
                                 tv[:, 768 * c:768 * c + 768],
                                 lg[:, 768 * c:768 * c + 768], AF.Tanh,
                                 deps=[t_log[2 * c], t_log[2 * c + 1],
                                       fr["tv"][par]])
                        for k2 in range(4):
                            s = 4 * c + k2
                            t_ex[s] = E("scalar", "activation",
                                        lg[:, 192 * s:192 * s + 192],
                                        tv[:, 192 * s:192 * s + 192], AF.Exp,
                                        scale=SOFTCAP,
                                        accum_out=srow[:, 32 + SIDX[s]:33 + SIDX[s]],
                                        deps=[t_th, fr["srow"][par]])
                        if ci == 1 or ci == 3:
                            h0 = 8 * (ci // 2)
                            t_rc[ci // 2] = E(
                                "vector", "reciprocal",
                                srow[:, 48 + h0:56 + h0],
                                srow[:, 32 + h0:40 + h0],
                                deps=[t_ex[4 * CH_ORDER[ci - 1] + k2]
                                      for k2 in range(4)]
                                + [t_ex[4 * c + k2] for k2 in range(4)]
                                + [fr["srow"][par]])
                            for cc in (CH_ORDER[ci - 1], c):
                                for k2 in range(4):
                                    s = 4 * cc + k2
                                    t_pr[s] = E(
                                        "vector", "tensor_scalar_mul",
                                        prob[:, s, :],
                                        lg[:, 192 * s:192 * s + 192],
                                        srow[:, 48 + SIDX[s]:49 + SIDX[s]],
                                        deps=[t_rc[ci // 2], t_ex[s],
                                              fr["prob"][par]])
                                    last_pr = t_pr[s]
                    fr["logt"][par] = last_pr
                    fr["tv"][par] = t_ex[4 * CH_ORDER[-1] + 3]
                    fr["srow"][par] = last_pr
                    t_pT = [None] * 8
                    tr_last = None
                    for pos, s2_ in enumerate(AC_ORDER):
                        bk = 4 + (pos & 1)
                        mms = []
                        for j in range(2):
                            s = 2 * s2_ + j
                            mms.append(E("tensor", "matmul",
                                         pb[bk][:, 256 * j:256 * j + 128],
                                         prob[:, s, 0:128], ident[:],
                                         start=True, stop=True,
                                         deps=([t_pr[s], t_id, pb_free[bk]]
                                               if j == 0 else [t_pr[s]])))
                            mms.append(E("tensor", "matmul",
                                         pb[bk][0:64, 256 * j + 128:256 * j + 256],
                                         prob[:, s, 128:192], ident[:],
                                         start=True, stop=True))
                        tr_last = mms[-1]
                        ev = E("scalar", "activation", pT[:, 2 * s2_:2 * s2_ + 2, :],
                               pb[bk][:, 0:512], AF.Copy, deps=[mms[-1]])                             if (pos & 2) else                             E("vector", "tensor_copy", pT[:, 2 * s2_:2 * s2_ + 2, :],
                              pb[bk][:, 0:512], deps=[mms[-1]])
                        pb_free[bk] = ev
                        t_pT[s2_] = ev
                    fr["prob"][par] = tr_last
                    t_avev = []
                    av_last = None
                    for hp in range(8):
                        bk = 6 + (hp & 1)
                        mlast = None
                        for h01 in range(2):
                            h = 2 * hp + h01
                            s = QR_SLOT[h]
                            tp = (0, 64) if h01 else None
                            pev = t_pT[s >> 1]
                            E("tensor", "matmul",
                              pb[bk][64 * h01:64 * h01 + 64, 0:128],
                              v0[:, 64 * h:64 * h + 64], pT[:, s, 0:128],
                              start=True, stop=False,
                              deps=([t_v0, pev, pb_free[bk]] if h01 == 0
                                    else [t_v0, pev]),
                              inc=False, tile_position=tp)
                            mlast = E("tensor", "matmul",
                                      pb[bk][64 * h01:64 * h01 + 64, 0:128],
                                      v1[0:64, 64 * h:64 * h + 64],
                                      pT[0:64, s, 128:256],
                                      start=False, stop=True,
                                      deps=[t_v1], tile_position=tp)
                        av_last = mlast
                        ev = E("scalar", "activation", ast[:, hp, :],
                               pb[bk][:, 0:128], AF.Copy,
                               deps=[mlast, st2[par]]) if (hp & 2) else                             E("vector", "tensor_copy", ast[:, hp, :],
                              pb[bk][:, 0:128], deps=[mlast, st2[par]])
                        pb_free[bk] = ev
                        t_avev.append(ev)
                    fr["v"][par] = av_last
                    t_ast = E("scalar", "dma_start",
                              attn_dram[:, :, 128 * g:128 * g + 128], ast[:],
                              dma=True, deps=t_avev)
                    st2[par] = t_ast

        sch.barrier()

        def emit_ln_stage(wname, w_dram, in_dram, in_is_hp8, res_dram, scale_v,
                          out_f32_dram, out_is_ext, tag, prefetch=()):
            ICN = 8 if in_is_hp8 else 32
            with ExitStack() as sctx:
                w_bf = sctx.enter_context(
                    nc.sbuf_tensor(f"w_{tag}", [128, ICN, HID], BF16))
                ing = [sctx.enter_context(
                    nc.sbuf_tensor(f"in_{tag}{i}", [128, ICN, TT], BF16))
                    for i in range(2)]
                zt = [sctx.enter_context(
                    nc.sbuf_tensor(f"z_{tag}{i}", [128, 8, TT], F32))
                    for i in range(3)]
                zbf = sctx.enter_context(nc.sbuf_tensor(f"zb_{tag}", [128, 2, TT], BF16))
                zsq = sctx.enter_context(nc.sbuf_tensor(f"zq_{tag}", [128, 2, TT], BF16))
                mu_t = sctx.enter_context(nc.sbuf_tensor(f"mu_{tag}", [128, TT], F32))
                e2_t = sctx.enter_context(nc.sbuf_tensor(f"e2_{tag}", [128, TT], F32))
                m2_t = sctx.enter_context(nc.sbuf_tensor(f"m2_{tag}", [128, TT], F32))

                t_w = E("gpsimd", "dma_start", w_bf[:],
                        w_dram.rearrange("(ic p) o -> p ic o", p=128), dma=True)
                pf_toks = [E("gpsimd", "dma_start", dst, src, dma=True)
                           for dst, src in prefetch]
                stf = [None, None, None]
                mm_free = [None, None]
                pend = None
                prev_ms = [None, None]
                prev_mq = [None, None]
                stats_free = None
                for it in range(NT + 1):
                    newpend = None
                    if it < NT:
                        par = it & 1
                        j = it % 3
                        b, t0 = it // 4, TT * (it % 4)
                        tk0 = 2048 * b + t0
                        ig = ing[par]
                        t_in = E("sync", "dma_start", ig[:],
                                 in_dram[:, :, tk0:tk0 + TT], dma=True,
                                 deps=[mm_free[par]])
                        if res_dram is x_fm:
                            t_res = E("sync", "dma_start", zt[j][:],
                                      x_fm[b].rearrange("(ic p) t -> p ic t", p=128)
                                      [:, :, t0:t0 + TT], dma=True,
                                      deps=[stf[j]])
                        else:
                            t_res = E("sync", "dma_start", zt[j][:],
                                      res_dram[:, :, tk0:tk0 + TT], dma=True,
                                      deps=[stf[j]])
                        t_z = []
                        for oc in range(8):
                            bk = oc & 1
                            mt = None
                            for ic in range(ICN):
                                mt = E("tensor", "matmul", pb[bk][:, 0:TT],
                                       w_bf[:, ic, 128 * oc:128 * oc + 128],
                                       ig[:, ic, :],
                                       start=(ic == 0), stop=(ic == ICN - 1),
                                       deps=[t_in, t_w, pb_free[bk]] if ic == 0 else [],
                                       inc=(ic == ICN - 1))
                            if scale_v is None:
                                ev = E("vector", "tensor_add", zt[j][:, oc, :],
                                       pb[bk][:, 0:TT], zt[j][:, oc, :],
                                       deps=[mt, t_res])
                            else:
                                ev = E("vector", "scalar_tensor_tensor",
                                       zt[j][:, oc, :],
                                       pb[bk][:, 0:TT], scale_v[:, oc:oc + 1],
                                       zt[j][:, oc, :], ALU.mult, ALU.add,
                                       deps=[mt, t_res])
                            pb_free[bk] = ev
                            t_z.append(ev)
                        mm_free[par] = mt
                        newpend = (it, j, t_z)
                    if pend is not None:
                        it0, j0, t_z = pend
                        b0, t00 = it0 // 4, TT * (it0 % 4)
                        tk00 = 2048 * b0 + t00
                        bkS, bkQ = (2, 3) if (it0 & 1) == 0 else (4, 5)
                        m_s = None
                        m_q = None
                        for oc in range(8):
                            t_cb = E("scalar", "activation", zbf[:, oc & 1, :],
                                     zt[j0][:, oc, :], AF.Copy,
                                     deps=[t_z[oc], prev_ms[oc & 1]])
                            t_cq = E("scalar", "activation", zsq[:, oc & 1, :],
                                     zt[j0][:, oc, :],
                                     AF.Square, deps=[t_z[oc], prev_mq[oc & 1]])
                            m_s = E("tensor", "matmul", pb[bkS][:, 0:TT],
                                    ones128[:], zbf[:, oc & 1, :],
                                    start=(oc == 0), stop=(oc == 7),
                                    deps=[t_cb, t_o128] + (
                                        [pb_free[bkS]] if oc == 0 else []),
                                    inc=True)
                            m_q = E("tensor", "matmul", pb[bkQ][:, 0:TT],
                                    ones128[:], zsq[:, oc & 1, :],
                                    start=(oc == 0), stop=(oc == 7),
                                    deps=[t_cq, t_o128] + (
                                        [pb_free[bkQ]] if oc == 0 else []),
                                    inc=True)
                            prev_ms[oc & 1] = m_s
                            prev_mq[oc & 1] = m_q
                        t_mu = E("vector", "tensor_scalar_mul", mu_t[:],
                                 pb[bkS][:, 0:TT], 1.0 / HID,
                                 deps=[m_s, stats_free])
                        pb_free[bkS] = t_mu
                        t_e2 = E("vector", "tensor_scalar_mul", e2_t[:],
                                 pb[bkQ][:, 0:TT], 1.0 / HID,
                                 deps=[m_q, stats_free])
                        pb_free[bkQ] = t_e2
                        t_m2 = E("vector", "tensor_tensor", m2_t[:],
                                 mu_t[:], mu_t[:], ALU.mult, deps=[t_mu])
                        t_var = E("vector", "tensor_sub", e2_t[:],
                                  e2_t[:], m2_t[:], deps=[t_e2, t_m2])
                        t_sd = E("scalar", "activation", m2_t[:], e2_t[:],
                                 AF.Ln, bias=epsv[:, 0:1],
                                 deps=[t_var, t_eps])
                        t_rs = E("scalar", "activation", e2_t[:], m2_t[:],
                                 AF.Exp, scale=-0.5,
                                 deps=[t_sd])
                        t_mr = E("vector", "tensor_tensor", mu_t[:],
                                 mu_t[:], e2_t[:], ALU.mult, deps=[t_rs])
                        t_o = []
                        for oc in range(8):
                            t_t1 = E("vector", "tensor_mul", zt[j0][:, oc, :],
                                     zt[j0][:, oc, :], e2_t[:],
                                     deps=[t_rs, t_z[oc]])
                            t_t2 = E("vector", "tensor_sub", zt[j0][:, oc, :],
                                     zt[j0][:, oc, :], mu_t[:],
                                     deps=[t_mr, t_t1])
                            t_o.append(t_t2)
                        stats_free = t_o[-1]
                        if out_is_ext:
                            st = E("gpsimd", "dma_start",
                                   out_fm[b0].rearrange(
                                       "(oc p) t -> p oc t", p=128)
                                   [:, :, t00:t00 + TT],
                                   zt[j0][:], dma=True, deps=t_o)
                        else:
                            st = E("gpsimd", "dma_start",
                                   out_f32_dram[:, :, tk00:tk00 + TT],
                                   zt[j0][:], dma=True, deps=t_o)
                        stf[j0] = st
                    pend = newpend
                return pf_toks

        s4ctx = ExitStack()
        if upto >= 4:
            w1b = s4ctx.enter_context(nc.sbuf_tensor("w1b", [128, 8, DH], BF16))
        if upto >= 3:
            pf = emit_ln_stage(
                "post", wpostT, attn_dram, True, x_fm, None,
                x1f_dram, False, "p",
                prefetch=([(w1b[:], w1T.rearrange("(ic p) o -> p ic o", p=128))]
                          if upto >= 4 else ()))
        else:
            pf = []

        sch.barrier()

        if upto >= 4:
            with s4ctx as sctx:
                w2b = sctx.enter_context(nc.sbuf_tensor("w2b", [128, 8, DH], BF16))
                x1g = [sctx.enter_context(nc.sbuf_tensor(f"x1g{i}", [128, 8, TT], BF16))
                       for i in range(2)]
                sil = sctx.enter_context(nc.sbuf_tensor("sil", [128, TT], F32))
                ast = sctx.enter_context(nc.sbuf_tensor("a_st", [128, 32, TT], BF16))
                t_w1 = pf[0] if pf else E(
                    "gpsimd", "dma_start", w1b[:],
                    w1T.rearrange("(ic p) o -> p ic o", p=128), dma=True)
                t_w2 = E("gpsimd", "dma_start", w2b[:],
                         w2T.rearrange("(ic p) o -> p ic o", p=128), dma=True)
                stf = [None, None]
                mm_free = [None, None]
                st_half = [None, None]
                for it in range(NT):
                    par = it & 1
                    b, t0 = it // 4, TT * (it % 4)
                    tk0 = 2048 * b + t0
                    xg = x1g[par]
                    t_in = E("gpsimd", "dma_start", xg[:],
                             x1f_dram[:, :, tk0:tk0 + TT], dma=True,
                             deps=[mm_free[par]])
                    t_as = []
                    prev_stt = None
                    for oc in range(32):
                        bk0, bk1 = (0, 1) if (oc & 1) == 0 else (2, 3)
                        m1 = None
                        for ic in range(8):
                            m1 = E("tensor", "matmul", pb[bk0][:, 0:TT],
                                   w1b[:, ic, 128 * oc:128 * oc + 128], xg[:, ic, :],
                                   start=(ic == 0), stop=(ic == 7),
                                   deps=[t_in, t_w1, pb_free[bk0]] if ic == 0 else [],
                                   inc=(ic == 7))
                        t_si = E("scalar", "activation", sil[:], pb[bk0][:, 0:TT],
                                 AF.Silu, scale=s1v[:, oc:oc + 1],
                                 deps=[m1, t_s1, prev_stt])
                        pb_free[bk0] = t_si
                        m2 = None
                        for ic in range(8):
                            m2 = E("tensor", "matmul", pb[bk1][:, 0:TT],
                                   w2b[:, ic, 128 * oc:128 * oc + 128], xg[:, ic, :],
                                   start=(ic == 0), stop=(ic == 7),
                                   deps=[t_in, t_w2, pb_free[bk1]] if ic == 0 else [],
                                   inc=(ic == 7))
                        ev = E("vector", "scalar_tensor_tensor", ast[:, oc, :],
                               pb[bk1][:, 0:TT], s2v[:, oc:oc + 1], sil[:],
                               ALU.mult, ALU.mult,
                               deps=[m2, t_si, t_s2, st_half[oc // 16]],
                               )
                        pb_free[bk1] = ev
                        prev_stt = ev
                        t_as.append(ev)
                        if oc == 15:
                            st_half[0] = E("scalar", "dma_start",
                                           a_dram[:, 0:16, tk0:tk0 + TT],
                                           ast[:, 0:16, :], dma=True,
                                           deps=t_as[0:16])
                    st_half[1] = E("scalar", "dma_start",
                                   a_dram[:, 16:32, tk0:tk0 + TT],
                                   ast[:, 16:32, :], dma=True, deps=t_as[16:32])
                    stf[par] = st_half[1]
                    mm_free[par] = m2

        sch.barrier()

        HPW = 256
        HNP = DH // HPW
        HST = 4
        HSUP = NT // HST

        def emit_h_stage(hT, hs_v, in_dram, out_dram, tag, bufs,
                         t_pn0=None, next_hT=None):
            panel, slab, bst = bufs
            hT_r = hT.rearrange("(ic p) o -> p ic o", p=128)
            stf = [None, None]
            pn_free = [None, None]
            t_pn = [None, None]
            t_pn[0] = t_pn0 if t_pn0 is not None else E(
                "gpsimd", "dma_start", panel[0][:], hT_r[:, :, 0:HPW],
                dma=True)
            t_sl = [None] * HST
            for j in range(HST):
                t_sl[j] = E("sync", "dma_start", slab[j][:],
                            in_dram[:, :, TT * j:TT * j + TT], dma=True)
            ret = None
            for sup in range(HSUP):
                for P in range(HNP):
                    pp = P & 1
                    last = (sup == HSUP - 1) and (P == HNP - 1)
                    if not last:
                        Pn = (P + 1) % HNP
                        t_pn[1 - pp] = E("gpsimd", "dma_start",
                                         panel[1 - pp][:],
                                         hT_r[:, :, HPW * Pn:HPW * Pn + HPW],
                                         dma=True, deps=[pn_free[1 - pp]])
                    elif next_hT is not None:
                        nhT_r = next_hT.rearrange("(ic p) o -> p ic o", p=128)
                        ret = E("gpsimd", "dma_start", panel[0][:],
                                nhT_r[:, :, 0:HPW], dma=True,
                                deps=[pn_free[0]])
                    last_mm = None
                    for j in range(HST):
                        it = HST * sup + j
                        tk0 = TT * it
                        evs = []
                        for oc in range(2):
                            bk = (2 * j + oc) & 7
                            mt = None
                            for ic in range(32):
                                mt = E("tensor", "matmul", pb[bk][:, 0:TT],
                                       panel[pp][:, ic, 128 * oc:128 * oc + 128],
                                       slab[j][:, ic, :],
                                       start=(ic == 0), stop=(ic == 31),
                                       deps=([t_sl[j], t_pn[pp], pb_free[bk]]
                                             if ic == 0 else []),
                                       inc=(ic == 31))
                            ev = E("scalar", "activation",
                                   bst[j & 1][:, oc, :],
                                   pb[bk][:, 0:TT], AF.Gelu,
                                   scale=hs_v[:, 2 * P + oc:2 * P + oc + 1],
                                   deps=[mt, stf[j & 1]])
                            pb_free[bk] = ev
                            evs.append(ev)
                            last_mm = mt
                        stf[j & 1] = E("scalar", "dma_start",
                                       out_dram[:, 2 * P:2 * P + 2,
                                                tk0:tk0 + TT],
                                       bst[j & 1][:], dma=True, deps=evs)
                        if P == HNP - 1 and sup + 1 < HSUP:
                            it2 = HST * (sup + 1) + j
                            t_sl[j] = E("sync", "dma_start", slab[j][:],
                                        in_dram[:, :, TT * it2:TT * it2 + TT],
                                        dma=True, deps=[last_mm])
                    pn_free[pp] = last_mm
            return ret

        if upto >= 5 or upto >= 6:
            with ExitStack() as hctx:
                hbufs = (
                    [hctx.enter_context(
                        nc.sbuf_tensor(f"pn_h{i}", [128, 32, HPW], BF16))
                     for i in range(2)],
                    [hctx.enter_context(
                        nc.sbuf_tensor(f"sl_h{j}", [128, 32, TT], BF16))
                     for j in range(HST)],
                    [hctx.enter_context(
                        nc.sbuf_tensor(f"bs_h{i}", [128, 2, TT], BF16))
                     for i in range(2)],
                )
                t_pn2 = None
                if upto >= 5:
                    t_pn2 = emit_h_stage(h1T, h1v, a_dram, b_dram, "h1", hbufs,
                                         next_hT=(h2T if upto >= 6 else None))
                sch.barrier()
                if upto >= 6:
                    emit_h_stage(h2T, h2v, b_dram, c_dram, "h2", hbufs,
                                 t_pn0=t_pn2)
        sch.barrier()

        if upto >= 7:
            emit_ln_stage("w3", w3T, c_dram, False, x1f_dram, w3v,
                          None, True, "y")

        sch.flush()
        sch.final_wait()

    return nc



def kernel(L0_attn_hidden_states, L0_attn_position_embeddings, L0_attn_w_q,
           L0_attn_w_k, L0_attn_w_v, L0_attn_w_rel, L0_attn_w_post,
           L0_attn_per_dim_scale, L0_ffn_w1_w, L0_ffn_w1_s, L0_ffn_w2_w,
           L0_ffn_w2_s, L0_ffn_h1_w, L0_ffn_h1_s, L0_ffn_h2_w, L0_ffn_h2_s,
           L0_ffn_w3_w, L0_ffn_w3_s, _upto=99, _dump=(), _nc=None):
    f32 = np.float32
    x = np.asarray(L0_attn_hidden_states, f32)
    B = x.shape[0]
    n_cores = 8
    bpc = B // n_cores

    def T(a):
        return np.ascontiguousarray(np.asarray(a, f32).T)

    common = {
        "wqT": T(L0_attn_w_q), "wkT": T(L0_attn_w_k), "wvT": T(L0_attn_w_v),
        "wrelT": T(L0_attn_w_rel), "wpostT": T(L0_attn_w_post),
        "posT": T(L0_attn_position_embeddings),
        "pds": np.ascontiguousarray(np.asarray(L0_attn_per_dim_scale, f32)),
        "w1T": T(L0_ffn_w1_w), "w2T": T(L0_ffn_w2_w),
        "h1T": T(L0_ffn_h1_w), "h2T": T(L0_ffn_h2_w), "w3T": T(L0_ffn_w3_w),
        "s1": np.ascontiguousarray(np.asarray(L0_ffn_w1_s, f32).reshape(32, 128).T),
        "s2": np.ascontiguousarray(np.asarray(L0_ffn_w2_s, f32).reshape(32, 128).T),
        "h1s": np.ascontiguousarray(np.asarray(L0_ffn_h1_s, f32).reshape(32, 128).T),
        "h2s": np.ascontiguousarray(np.asarray(L0_ffn_h2_s, f32).reshape(32, 128).T),
        "w3s": np.ascontiguousarray(np.asarray(L0_ffn_w3_s, f32).reshape(8, 128).T),
    }
    x_fm = np.ascontiguousarray(x.transpose(0, 2, 1))
    in_maps = []
    for c in range(n_cores):
        m = dict(common)
        m["x_fm"] = np.ascontiguousarray(x_fm[c * bpc:(c + 1) * bpc])
        in_maps.append(m)

    nc = _nc if _nc is not None else build_program(upto=_upto, dump=_dump)
    res = run_bass_kernel_spmd(nc, in_maps, core_ids=list(range(n_cores)))
    if _dump:
        return res
    out = np.concatenate([r["out_fm"] for r in res.results], axis=0)
    return np.ascontiguousarray(out.transpose(0, 2, 1))


# version 14 (final)
# speedup vs baseline: 1.0973x; 1.0210x over previous
import math
import os
from contextlib import ExitStack
S1_MODE = os.environ.get("K_S1_MODE", "full")
S1_NT = int(os.environ.get("K_S1_NT", "999"))
S1_QRH = int(os.environ.get("K_S1_QRH", "16"))

import numpy as np

import concourse.bass as bass
import concourse.mybir as mybir
from concourse.bass_utils import run_bass_kernel_spmd
from concourse.ap import AP

F32 = mybir.dt.float32
BF16 = mybir.dt.bfloat16
AF = mybir.ActivationFunctionType
ALU = mybir.AluOpType
AX = mybir.AxisListType

H = 16
D = 64
HID = 1024
DH = 4096
CHUNK = 32
PAST, FUT = 31, 8
CTX = 71
SEQ = 2048
NSEQ = 4
NTOK = NSEQ * SEQ
TT = 512
NT = NTOK // TT
NGS = SEQ // 128
NG = NSEQ * NGS
KPAD = SEQ + 64
GROW = 72 * 128
HSTR = 1 + NTOK * 72 + 7

QR_ORDER = [0, 2, 4, 6, 8, 10, 12, 14, 1, 3, 5, 7, 9, 11, 13, 15]
QR_SLOT = {h: i for i, h in enumerate(QR_ORDER)}

Q_SCALE = D ** (-0.5) / math.log(2)
K_SCALE = math.log(1.0 + math.e) / math.log(2)
SOFTCAP = 50.0

STRICT_SAME_ENGINE = True

ENGINES = ("tensor", "vector", "scalar", "gpsimd", "sync")
DMA_QUEUES = ("gpsimd", "sync", "scalar")
NDMA_SLOTS = 4


class Sched:

    def __init__(self, nc, ctx, block):
        self.nc = nc
        self.block = block
        names = list(ENGINES)
        for q in DMA_QUEUES:
            for s in range(NDMA_SLOTS):
                names.append(f"dq_{q}{s}")
        self.sems = {n: ctx.enter_context(nc.semaphore(f"s_{n}")) for n in names}
        self.count = {n: 0 for n in names}
        self.queues = {e: [] for e in ENGINES}
        self.dma_uses = {q: 0 for q in DMA_QUEUES}
        self.last_wait = {e: {n: 0 for n in names} for e in ENGINES}
        self.pending_barrier = {e: {} for e in ENGINES}

    def emit(self, eng, fn_name, *args, deps=(), inc=True, dma=False, **kwargs):
        if dma:
            slot = self.dma_uses[eng] % NDMA_SLOTS
            self.dma_uses[eng] += 1
            sem_name = f"dq_{eng}{slot}"
        else:
            sem_name = eng
        waits = dict(self.pending_barrier[eng])
        self.pending_barrier[eng] = {}
        if dma and self.count[sem_name] > 0:
            if self.count[sem_name] > waits.get(sem_name, 0):
                waits[sem_name] = self.count[sem_name]
        for tok in deps:
            if tok is None:
                continue
            n, v = tok
            if n == eng and not STRICT_SAME_ENGINE:
                continue
            if v > waits.get(n, 0):
                waits[n] = v
        real_waits = []
        for n, v in waits.items():
            if v > self.last_wait[eng][n]:
                real_waits.append((n, v))
                self.last_wait[eng][n] = v
        tok = None
        if inc:
            self.count[sem_name] += 16 if dma else 1
            tok = (sem_name, self.count[sem_name])

        def step(engobj, fn_name=fn_name, args=args, kwargs=kwargs,
                 real_waits=real_waits, inc=inc, dma=dma, sem_name=sem_name):
            for n, v in real_waits:
                engobj.wait_ge(self.sems[n], v)
            inst = getattr(engobj, fn_name)(*args, **kwargs)
            if inc:
                inst.then_inc(self.sems[sem_name], 16 if dma else 1)

        self.queues[eng].append(step)
        return tok

    def raw(self, eng, fn):
        self.queues[eng].append(fn)

    def barrier(self):
        snap = {n: v for n, v in self.count.items() if v > 0}
        for e in ENGINES:
            merged = dict(self.pending_barrier[e])
            for n, v in snap.items():
                if v > merged.get(n, 0):
                    merged[n] = v
            self.pending_barrier[e] = merged

    def flush(self):
        block = self.block
        for e in ENGINES:
            q = self.queues[e]
            if not q:
                continue
            self.queues[e] = []
            section = getattr(block, e)

            def body(engobj, q=q):
                for step in q:
                    step(engobj)

            section(body)

    def final_wait(self):
        snap = {n: v for n, v in self.count.items() if v > 0}

        def body(engobj):
            for n, v in snap.items():
                engobj.wait_ge(self.sems[n], v)

        self.block.sync(body)


def build_program(upto=99, dump=()):
    nc = bass.Bass("TRN2", target_bir_lowering=False, debug=False)

    def dram(name, shape, dtype, kind="Internal"):
        k = "ExternalOutput" if name in dump else kind
        return nc.dram_tensor(name, shape, dtype, kind=k).ap()

    x_fm = nc.dram_tensor("x_fm", [NSEQ, HID, SEQ], F32, kind="ExternalInput").ap()
    wqT = nc.dram_tensor("wqT", [HID, HID], F32, kind="ExternalInput").ap()
    wkT = nc.dram_tensor("wkT", [HID, HID], F32, kind="ExternalInput").ap()
    wvT = nc.dram_tensor("wvT", [HID, HID], F32, kind="ExternalInput").ap()
    wrelT = nc.dram_tensor("wrelT", [HID, HID], F32, kind="ExternalInput").ap()
    wpostT = nc.dram_tensor("wpostT", [HID, HID], F32, kind="ExternalInput").ap()
    posT = nc.dram_tensor("posT", [HID, CTX], F32, kind="ExternalInput").ap()
    pdsd = nc.dram_tensor("pds", [D], F32, kind="ExternalInput").ap()
    w1T = nc.dram_tensor("w1T", [HID, DH], F32, kind="ExternalInput").ap()
    w2T = nc.dram_tensor("w2T", [HID, DH], F32, kind="ExternalInput").ap()
    h1T = nc.dram_tensor("h1T", [DH, DH], F32, kind="ExternalInput").ap()
    h2T = nc.dram_tensor("h2T", [DH, DH], F32, kind="ExternalInput").ap()
    w3T = nc.dram_tensor("w3T", [DH, HID], F32, kind="ExternalInput").ap()
    s1d = nc.dram_tensor("s1", [128, 32], F32, kind="ExternalInput").ap()
    s2d = nc.dram_tensor("s2", [128, 32], F32, kind="ExternalInput").ap()
    h1sd = nc.dram_tensor("h1s", [128, 32], F32, kind="ExternalInput").ap()
    h2sd = nc.dram_tensor("h2s", [128, 32], F32, kind="ExternalInput").ap()
    w3sd = nc.dram_tensor("w3s", [128, 8], F32, kind="ExternalInput").ap()

    out_fm = nc.dram_tensor("out_fm", [NSEQ, HID, SEQ], F32,
                            kind="ExternalOutput").ap()

    q_dram = dram("q_dram", [128, 8, NTOK], BF16)
    k_dram = dram("k_dram", [128, 8, NSEQ * KPAD], BF16)
    v_dram = dram("v_dram", [NSEQ * KPAD, HID], BF16)
    g_dram = dram("g_dram", [H, HSTR], F32)
    attn_dram = dram("attn_dram", [128, 8, NTOK], BF16)
    x1f_dram = dram("x1f_dram", [128, 8, NTOK], F32)
    a_dram = dram("a_dram", [128, 32, NTOK], BF16)
    b_dram = dram("b_dram", [128, 32, NTOK], BF16)
    c_dram = dram("c_dram", [128, 32, NTOK], BF16)

    ctx0 = ExitStack()
    with ctx0:
        block = ctx0.enter_context(nc.Block())
        sch = Sched(nc, ctx0, block)

        ident = ctx0.enter_context(nc.sbuf_tensor("ident", [128, 128], BF16))
        sq128 = ctx0.enter_context(nc.sbuf_tensor("sq128", [128, 8], F32))
        relk = ctx0.enter_context(nc.sbuf_tensor("relk", [128, 8, 72], BF16))
        ones_bf = ctx0.enter_context(nc.sbuf_tensor("ones_bf", [128, 2], BF16))
        ones128 = ctx0.enter_context(nc.sbuf_tensor("ones128", [128, 128], BF16))
        ones_f1 = ctx0.enter_context(nc.sbuf_tensor("ones_f1", [1, 128], F32))
        pb = [ctx0.enter_context(nc.psum_tensor(f"pb{i}", [128, 512], F32))
              for i in range(8)]
        pb_free = {i: None for i in range(8)}

        s1v = ctx0.enter_context(nc.sbuf_tensor("s1v", [128, 32], F32))
        s2v = ctx0.enter_context(nc.sbuf_tensor("s2v", [128, 32], F32))
        h1v = ctx0.enter_context(nc.sbuf_tensor("h1v", [128, 32], F32))
        h2v = ctx0.enter_context(nc.sbuf_tensor("h2v", [128, 32], F32))
        w3v = ctx0.enter_context(nc.sbuf_tensor("w3v", [128, 8], F32))
        zeros = ctx0.enter_context(nc.sbuf_tensor("zeros", [128, 1024], BF16))
        zerof = ctx0.enter_context(nc.sbuf_tensor("zerof", [1, 16], F32))
        epsv = ctx0.enter_context(nc.sbuf_tensor("epsv", [128, 1], F32))

        E = sch.emit

        for i in range(8):
            pb_free[i] = E("vector", "memset", pb[i][:], 0.0)
        t_id0 = E("gpsimd", "memset", ident[:], 0.0)
        t_id = E("gpsimd", "affine_select", out=ident[:], in_=ident[:],
                 compare_op=ALU.not_equal, fill=1.0, base=0,
                 pattern=[[-1, 128]], channel_multiplier=1, deps=[t_id0])
        t_ones = E("gpsimd", "memset", ones_bf[:], 1.0)
        t_o128 = E("gpsimd", "memset", ones128[:], 1.0)
        t_ones1 = E("gpsimd", "memset", ones_f1[:], 1.0)
        t_zero = E("gpsimd", "memset", zeros[:], 0.0)
        t_zerof = E("gpsimd", "memset", zerof[:], 0.0)
        t_eps = E("gpsimd", "memset", epsv[:], 1e-5)

        t_s1 = E("sync", "dma_start", s1v[:], s1d, dma=True)
        t_s2 = E("sync", "dma_start", s2v[:], s2d, dma=True)
        t_h1 = E("sync", "dma_start", h1v[:], h1sd, dma=True)
        t_h2 = E("sync", "dma_start", h2v[:], h2sd, dma=True)
        t_w3 = E("sync", "dma_start", w3v[:], w3sd, dma=True)

        pds_ap = pdsd.rearrange("(p o) -> p o", o=1)
        t_p0 = E("sync", "dma_start", sq128[0:64, 1:2], pds_ap, dma=True)
        t_p1 = E("sync", "dma_start", sq128[64:128, 1:2], pds_ap, dma=True)
        t_a = E("scalar", "activation", sq128[:, 2:3], sq128[:, 1:2], AF.Abs,
                deps=[t_p0, t_p1])
        t_e = E("scalar", "activation", sq128[:, 3:4], sq128[:, 2:3], AF.Exp,
                scale=-1.0, deps=[t_a])
        t_l = E("scalar", "activation", sq128[:, 4:5], sq128[:, 3:4], AF.Ln,
                bias=1.0, deps=[t_e])
        t_m = E("vector", "tensor_scalar_max", sq128[:, 5:6], sq128[:, 1:2],
                0.0, deps=[t_p0, t_p1])
        t_sp = E("vector", "tensor_add", sq128[:, 6:7], sq128[:, 5:6],
                 sq128[:, 4:5], deps=[t_m, t_l])
        t_sq = E("vector", "tensor_scalar_mul", sq128[:, 0:1], sq128[:, 6:7],
                 Q_SCALE, deps=[t_sp])

        pad_toks = []
        for b in range(NSEQ):
            for off in (0, KPAD - 32):
                pad_toks.append(E("sync", "dma_start",
                                  k_dram[:, :, b * KPAD + off:b * KPAD + off + 32],
                                  zeros[:, 0:256].rearrange("p (a w) -> p a w", a=8),
                                  dma=True, deps=[t_zero]))
                pad_toks.append(E("sync", "dma_start",
                                  v_dram[b * KPAD + off:b * KPAD + off + 32, :],
                                  zeros[0:32, :], dma=True, deps=[t_zero]))
        t_guard = E("sync", "dma_start", g_dram[:, 0:1], zerof[0:1, :],
                    dma=True, deps=[t_zerof], allow_slow_non_contiguous=True)

        if upto >= 1:
            with ExitStack() as sctx:
                wq_bf = sctx.enter_context(nc.sbuf_tensor("wq_bf", [128, 8, HID], BF16))
                wk_bf = sctx.enter_context(nc.sbuf_tensor("wk_bf", [128, 8, HID], BF16))
                wv_bf = sctx.enter_context(nc.sbuf_tensor("wv_bf", [128, 8, HID], BF16))
                wr_bf = sctx.enter_context(nc.sbuf_tensor("wr_bf", [128, 8, HID], BF16))
                posb = sctx.enter_context(nc.sbuf_tensor("posb", [128, 8, 72], BF16))
                xbf = [sctx.enter_context(nc.sbuf_tensor(f"xbf{i}", [128, 8, TT], BF16))
                       for i in range(2)]
                qst = [sctx.enter_context(nc.sbuf_tensor(f"qst{i}", [128, 8, TT], BF16))
                       for i in range(2)]
                kst = [sctx.enter_context(nc.sbuf_tensor(f"kst{i}", [128, 8, TT], BF16))
                       for i in range(2)]
                vst = [sctx.enter_context(nc.sbuf_tensor(f"vst{i}", [128, 4, HID], BF16))
                       for i in range(2)]
                gst = [sctx.enter_context(nc.sbuf_tensor(f"gst{i}", [128, 4, H, 72], F32))
                       for i in range(2)]

                def wload(dst, src):
                    return E("gpsimd", "dma_start", dst[:],
                             src.rearrange("(ic p) o -> p ic o", p=128), dma=True)

                t_wq = wload(wq_bf, wqT)
                t_wk = wload(wk_bf, wkT)
                t_wv = wload(wv_bf, wvT)
                t_wr = wload(wr_bf, wrelT)
                t_posz = E("gpsimd", "memset", posb[:], 0.0)
                t_pos = E("gpsimd", "dma_start", posb[:, :, 0:71],
                          posT.rearrange("(ic p) l -> p ic l", p=128),
                          dma=True, deps=[t_posz])

                t_relk = []
                for oc in range(8 if S1_MODE != "w" else 0):
                    bk = 6 + (oc & 1)
                    mt = None
                    for ic in range(8):
                        mt = E("tensor", "matmul", pb[bk][:, 0:72],
                               wr_bf[:, ic, 128 * oc:128 * oc + 128],
                               posb[:, ic, :],
                               start=(ic == 0), stop=(ic == 7),
                               deps=[t_wr, t_pos, pb_free[bk]] if ic == 0 else [],
                               inc=(ic == 7))
                    ev = E("vector", "tensor_copy", relk[:, oc, :], pb[bk][:, 0:72],
                           deps=[mt])
                    pb_free[bk] = ev
                    t_relk.append(ev)

                qkv_done = [[None] * 3 for _ in range(NT)]
                g_done = [None] * NT
                st_free = [None, None]
                for it in range(min(NT, S1_NT) if S1_MODE not in ("w", "relk") else 0):
                    par = it & 1
                    b, t0 = it // 4, TT * (it % 4)
                    xb, qs, ks, vs, gs = xbf[par], qst[par], kst[par], vst[par], gst[par]
                    free_dep = st_free[par] or []
                    t_x = E("gpsimd", "dma_start", xb[:],
                            x_fm[b].rearrange("(ic p) t -> p ic t", p=128)
                            [:, :, t0:t0 + TT],
                            dma=True, deps=[t_wq] + list(free_dep))
                    ev_q = []
                    ev_k = []
                    for proj, wbf, stg, evl in (
                            ("q", wq_bf, qs, ev_q), ("k", wk_bf, ks, ev_k))[
                                :0 if S1_MODE == "x" else 2]:
                        for oc in range(8):
                            bk = oc & 1
                            mt = None
                            for ic in range(8):
                                mt = E("tensor", "matmul", pb[bk][:, 0:TT],
                                       wbf[:, ic, 128 * oc:128 * oc + 128],
                                       xb[:, ic, :],
                                       start=(ic == 0), stop=(ic == 7),
                                       deps=[t_x, pb_free[bk]] if ic == 0 else [],
                                       inc=(ic == 7))
                            if proj == "q":
                                ev = E("scalar", "activation", stg[:, oc, :],
                                       pb[bk][:, 0:TT], AF.Copy,
                                       scale=sq128[:, 0:1], deps=[mt, t_sq])
                            else:
                                ev = E("scalar", "activation", stg[:, oc, :],
                                       pb[bk][:, 0:TT], AF.Copy,
                                       scale=K_SCALE, deps=[mt])
                            pb_free[bk] = ev
                            evl.append(ev)
                    ev_v = []
                    for sub in range(4 if S1_MODE in ("full", "v", "qr") else 0):
                        for nh in range(2):
                            bk = 2 + nh
                            mt = None
                            for ic in range(8):
                                mt = E("tensor", "matmul", pb[bk][:, 0:512],
                                       xb[:, ic, 128 * sub:128 * sub + 128],
                                       wv_bf[:, ic, 512 * nh:512 * nh + 512],
                                       start=(ic == 0), stop=(ic == 7),
                                       deps=[t_x, t_wv, pb_free[bk]] if ic == 0 else [],
                                       inc=(ic == 7))
                            ev = E("scalar", "activation",
                                   vs[:, sub, 512 * nh:512 * nh + 512],
                                   pb[bk][:, 0:512], AF.Copy, deps=[mt])
                            pb_free[bk] = ev
                            ev_v.append(ev)
                    ev_g = []
                    for sub in range(4 if S1_MODE in ("full", "qr", "qr_nostore", "qr_noevac") else 0):
                        n_qr = min(16, S1_QRH)
                        for q4 in range(4):
                            heads = [h for h in QR_ORDER[4 * q4:4 * q4 + 4]
                                     if h < n_qr or S1_QRH >= 16]
                            if not heads:
                                continue
                            bk = 4 + q4
                            mts = []
                            for j, h in enumerate(heads):
                                mts.append(E(
                                    "tensor", "matmul",
                                    pb[bk][:, 128 * j:128 * j + 72],
                                    qs[64 * (h & 1):64 * (h & 1) + 64,
                                       h >> 1, 128 * sub:128 * sub + 128],
                                    relk[64 * (h & 1):64 * (h & 1) + 64, h >> 1, :],
                                    start=True, stop=True,
                                    deps=([ev_q[h >> 1], t_relk[h >> 1],
                                           pb_free[bk]] if j == 0 else
                                          [ev_q[h >> 1], t_relk[h >> 1]]),
                                    inc=(j == len(heads) - 1)))
                            if S1_MODE == "qr_noevac":
                                pb_free[bk] = mts[-1]
                                continue
                            ev = E("scalar", "activation",
                                   gs[:, sub, 4 * q4:4 * q4 + len(heads), :],
                                   pb[bk][:, 0:512].rearrange(
                                       "p (a w) -> p a w", a=4)[:, 0:len(heads), 0:72],
                                   AF.Copy,
                                   scale=1.0 / SOFTCAP, deps=[mts[-1]])
                            pb_free[bk] = ev
                            ev_g.append((sub, ev))
                    all_st = []
                    if ev_q:
                        tq = E("sync", "dma_start",
                               q_dram[:, :, 2048 * b + t0:2048 * b + t0 + TT],
                               qs[:], dma=True, deps=ev_q)
                        tk = E("sync", "dma_start",
                               k_dram[:, :, KPAD * b + 32 + t0:KPAD * b + 32 + t0 + TT],
                               ks[:], dma=True, deps=ev_k)
                        all_st += [tq, tk]
                    tvs = []
                    for sub in range(4 if ev_v else 0):
                        r0 = KPAD * b + 32 + t0 + 128 * sub
                        tvs.append(E("sync", "dma_start", v_dram[r0:r0 + 128, :],
                                     vs[:, sub, :], dma=True, deps=[ev_v[2 * sub],
                                                                    ev_v[2 * sub + 1]]))
                    all_st += tvs
                    tgs = []
                    for sub in range(4 if (ev_g and S1_MODE != "qr_nostore") else 0):
                        gi = 4 * it + sub
                        dst = AP(g_dram.tensor, 1 + GROW * gi,
                                 [[72, 128], [HSTR, H], [1, 72]])
                        tgs.append(E("sync", "dma_start", dst, gs[:, sub],
                                     dma=True,
                                     deps=[ev for s, ev in ev_g if s == sub]))
                    all_st += tgs
                    qkv_done[it] = [None, None, tvs]
                    g_done[it] = tgs
                    st_free[par] = all_st

        sch.barrier()

        if upto >= 2:
            with ExitStack() as sctx:
                q_g = [sctx.enter_context(nc.sbuf_tensor(f"q_g{i}", [128, 8, 128], BF16))
                       for i in range(3)]
                k_g = [sctx.enter_context(nc.sbuf_tensor(f"k_g{i}", [128, 8, 192], BF16))
                       for i in range(3)]
                v_g0 = [sctx.enter_context(nc.sbuf_tensor(f"v_g0{i}", [128, HID], BF16))
                        for i in range(3)]
                v_g1 = [sctx.enter_context(nc.sbuf_tensor(f"v_g1{i}", [64, HID], BF16))
                        for i in range(3)]
                bbt = [sctx.enter_context(nc.sbuf_tensor(f"bb{i}", [128, H * 192], F32))
                       for i in range(3)]
                logt = [sctx.enter_context(nc.sbuf_tensor(f"logt{i}", [128, H * 192], F32))
                        for i in range(3)]
                tvt = [sctx.enter_context(nc.sbuf_tensor(f"tvt{i}", [128, H * 192], F32))
                       for i in range(3)]
                probt = [sctx.enter_context(nc.sbuf_tensor(f"prob{i}", [128, H, 192], BF16))
                         for i in range(3)]
                pTt = [sctx.enter_context(nc.sbuf_tensor(f"pT{i}", [128, H, 256], BF16))
                       for i in range(3)]
                srowt = [sctx.enter_context(nc.sbuf_tensor(f"srow{i}", [128, 64], F32))
                         for i in range(3)]
                atst = [sctx.enter_context(nc.sbuf_tensor(f"atst{i}", [128, 8, 128], BF16))
                        for i in range(3)]
                fr = {k2: [None, None, None] for k2 in
                      ("qk", "v", "bb", "logt", "tv", "srow", "prob", "ast")}
                st2 = [None, None, None]
                t_bbinit = [E("gpsimd", "memset", bbt[i][:], -1.0e9)
                            for i in range(3)]
                AC_ORDER = [0, 4, 1, 5, 2, 6, 3, 7]
                CH_ORDER = [0, 2, 1, 3]
                SIDX = {}
                for ci, c in enumerate(CH_ORDER):
                    for k2 in range(4):
                        SIDX[4 * c + k2] = 4 * ci + k2
                pend2 = None
                for g in range(NG + 1):
                    cur = None
                    if g < NG:
                        par = g % 3
                        b, gp = g // NGS, g % NGS
                        qg, kg, v0, v1 = q_g[par], k_g[par], v_g0[par], v_g1[par]
                        bb, lg, tv, prob = bbt[par], logt[par], tvt[par], probt[par]
                        pT, srow, ast = pTt[par], srowt[par], atst[par]
                        t_q = E("sync", "dma_start", qg[:],
                                q_dram[:, :, 128 * g:128 * g + 128], dma=True,
                                deps=[fr["qk"][par]])
                        t_k = E("sync", "dma_start", kg[:],
                                k_dram[:, :, KPAD * b + 128 * gp:KPAD * b + 128 * gp + 192],
                                dma=True, deps=[fr["qk"][par]])
                        t_v0 = E("sync", "dma_start", v0[:],
                                 v_dram[KPAD * b + 128 * gp:KPAD * b + 128 * gp + 128, :],
                                 dma=True, deps=[fr["v"][par]])
                        t_v1 = E("sync", "dma_start", v1[:],
                                 v_dram[KPAD * b + 128 * gp + 128:KPAD * b + 128 * gp + 192, :],
                                 dma=True, deps=[fr["v"][par]])
                        bb3 = bb.rearrange("p (h c) -> p h c", c=192)
                        t_bbs = []
                        for m in range(4):
                            srcap = AP(g_dram.tensor, GROW * g + 2304 * m + 1,
                                       [[71, 32], [HSTR, H], [1, 71]])
                            t_bbs.append(E(
                                "gpsimd", "dma_start",
                                bb3[32 * m:32 * m + 32, :, 32 * m + 1:32 * m + 72],
                                srcap, dma=True,
                                deps=[fr["bb"][par], t_bbinit[par]]))
                        t_bb = t_bbs[-1]

                        t_log = [None] * 8
                        ac_last = None
                        for pos, i2 in enumerate(AC_ORDER):
                            ha, hb_ = QR_ORDER[2 * i2], QR_ORDER[2 * i2 + 1]
                            bk = pos % 4
                            base = 64 * (ha & 1)
                            deps0 = [t_q, t_k, pb_free[bk]]
                            m_a = E("tensor", "matmul", pb[bk][:, 0:192],
                                    qg[base:base + 64, ha >> 1, :],
                                    kg[base:base + 64, ha >> 1, :],
                                    start=True, stop=True, deps=deps0)
                            m_b = E("tensor", "matmul", pb[bk][:, 192:384],
                                    qg[base:base + 64, hb_ >> 1, :],
                                    kg[base:base + 64, hb_ >> 1, :],
                                    start=True, stop=True)
                            ac_last = m_b
                            ev = E("vector", "scalar_tensor_tensor",
                                   lg[:, 384 * i2:384 * i2 + 384],
                                   pb[bk][:, 0:384], 1.0 / SOFTCAP,
                                   bb[:, 384 * i2:384 * i2 + 384],
                                   ALU.mult, ALU.add,
                                   deps=[m_b, fr["logt"][par]] + t_bbs)
                            pb_free[bk] = ev
                            t_log[i2] = ev
                        fr["qk"][par] = ac_last
                        fr["bb"][par] = t_log[AC_ORDER[-1]]
                        cur = (g, par, t_log, t_v0, t_v1)
                    if pend2 is not None:
                        g0, par, t_log, t_v0, t_v1 = pend2
                        qg, kg, v0, v1 = q_g[par], k_g[par], v_g0[par], v_g1[par]
                        bb, lg, tv, prob = bbt[par], logt[par], tvt[par], probt[par]
                        pT, srow, ast = pTt[par], srowt[par], atst[par]
                        t_ex = [None] * H
                        t_rc = [None, None]
                        t_pr = [None] * H
                        last_pr = None
                        for ci, c in enumerate(CH_ORDER):
                            t_th = E("scalar", "activation",
                                     tv[:, 768 * c:768 * c + 768],
                                     lg[:, 768 * c:768 * c + 768], AF.Tanh,
                                     deps=[t_log[2 * c], t_log[2 * c + 1],
                                           fr["tv"][par]])
                            for k2 in range(4):
                                s = 4 * c + k2
                                t_ex[s] = E("scalar", "activation",
                                            lg[:, 192 * s:192 * s + 192],
                                            tv[:, 192 * s:192 * s + 192], AF.Exp,
                                            scale=SOFTCAP,
                                            accum_out=srow[:, 32 + SIDX[s]:33 + SIDX[s]],
                                            deps=[t_th, fr["srow"][par]])
                            if ci == 1 or ci == 3:
                                h0 = 8 * (ci // 2)
                                t_rc[ci // 2] = E(
                                    "vector", "reciprocal",
                                    srow[:, 48 + h0:56 + h0],
                                    srow[:, 32 + h0:40 + h0],
                                    deps=[t_ex[4 * CH_ORDER[ci - 1] + k2]
                                          for k2 in range(4)]
                                    + [t_ex[4 * c + k2] for k2 in range(4)]
                                    + [fr["srow"][par]])
                                for cc in (CH_ORDER[ci - 1], c):
                                    for k2 in range(4):
                                        s = 4 * cc + k2
                                        t_pr[s] = E(
                                            "vector", "tensor_scalar_mul",
                                            prob[:, s, :],
                                            lg[:, 192 * s:192 * s + 192],
                                            srow[:, 48 + SIDX[s]:49 + SIDX[s]],
                                            deps=[t_rc[ci // 2], t_ex[s],
                                                  fr["prob"][par]])
                                        last_pr = t_pr[s]
                        fr["logt"][par] = last_pr
                        fr["tv"][par] = t_ex[4 * CH_ORDER[-1] + 3]
                        fr["srow"][par] = last_pr
                        t_pT = [None] * 8
                        tr_last = None
                        for pos, s2_ in enumerate(AC_ORDER):
                            bk = 4 + (pos & 1)
                            mms = []
                            for j in range(2):
                                s = 2 * s2_ + j
                                mms.append(E("tensor", "matmul",
                                             pb[bk][:, 256 * j:256 * j + 128],
                                             prob[:, s, 0:128], ident[:],
                                             start=True, stop=True,
                                             deps=([t_pr[s], t_id, pb_free[bk]]
                                                   if j == 0 else [t_pr[s]])))
                                mms.append(E("tensor", "matmul",
                                             pb[bk][0:64, 256 * j + 128:256 * j + 256],
                                             prob[:, s, 128:192], ident[:],
                                             start=True, stop=True))
                            tr_last = mms[-1]
                            ev = E("scalar", "activation", pT[:, 2 * s2_:2 * s2_ + 2, :],
                                   pb[bk][:, 0:512], AF.Copy, deps=[mms[-1]])                             if (pos & 2) else                             E("vector", "tensor_copy", pT[:, 2 * s2_:2 * s2_ + 2, :],
                                  pb[bk][:, 0:512], deps=[mms[-1]])
                            pb_free[bk] = ev
                            t_pT[s2_] = ev
                        fr["prob"][par] = tr_last
                        t_avev = []
                        av_last = None
                        for hp in range(8):
                            bk = 6 + (hp & 1)
                            mlast = None
                            for h01 in range(2):
                                h = 2 * hp + h01
                                s = QR_SLOT[h]
                                tp = (0, 64) if h01 else None
                                pev = t_pT[s >> 1]
                                E("tensor", "matmul",
                                  pb[bk][64 * h01:64 * h01 + 64, 0:128],
                                  v0[:, 64 * h:64 * h + 64], pT[:, s, 0:128],
                                  start=True, stop=False,
                                  deps=([t_v0, pev, pb_free[bk]] if h01 == 0
                                        else [t_v0, pev]),
                                  inc=False, tile_position=tp)
                                mlast = E("tensor", "matmul",
                                          pb[bk][64 * h01:64 * h01 + 64, 0:128],
                                          v1[0:64, 64 * h:64 * h + 64],
                                          pT[0:64, s, 128:256],
                                          start=False, stop=True,
                                          deps=[t_v1], tile_position=tp)
                            av_last = mlast
                            ev = E("scalar", "activation", ast[:, hp, :],
                                   pb[bk][:, 0:128], AF.Copy,
                                   deps=[mlast, st2[par]]) if (hp & 2) else                             E("vector", "tensor_copy", ast[:, hp, :],
                                  pb[bk][:, 0:128], deps=[mlast, st2[par]])
                            pb_free[bk] = ev
                            t_avev.append(ev)
                        fr["v"][par] = av_last
                        t_ast = E("scalar", "dma_start",
                                  attn_dram[:, :, 128 * g0:128 * g0 + 128], ast[:],
                                  dma=True, deps=t_avev)
                        st2[par] = t_ast

                    pend2 = cur
        sch.barrier()

        def emit_ln_stage(wname, w_dram, in_dram, in_is_hp8, res_dram, scale_v,
                          out_f32_dram, out_is_ext, tag, prefetch=()):
            ICN = 8 if in_is_hp8 else 32
            with ExitStack() as sctx:
                w_bf = sctx.enter_context(
                    nc.sbuf_tensor(f"w_{tag}", [128, ICN, HID], BF16))
                ing = [sctx.enter_context(
                    nc.sbuf_tensor(f"in_{tag}{i}", [128, ICN, TT], BF16))
                    for i in range(2)]
                zt = [sctx.enter_context(
                    nc.sbuf_tensor(f"z_{tag}{i}", [128, 8, TT], F32))
                    for i in range(3)]
                zbf = sctx.enter_context(nc.sbuf_tensor(f"zb_{tag}", [128, 2, TT], BF16))
                zsq = sctx.enter_context(nc.sbuf_tensor(f"zq_{tag}", [128, 2, TT], BF16))
                mu_t = sctx.enter_context(nc.sbuf_tensor(f"mu_{tag}", [128, TT], F32))
                e2_t = sctx.enter_context(nc.sbuf_tensor(f"e2_{tag}", [128, TT], F32))
                m2_t = sctx.enter_context(nc.sbuf_tensor(f"m2_{tag}", [128, TT], F32))

                t_w = E("gpsimd", "dma_start", w_bf[:],
                        w_dram.rearrange("(ic p) o -> p ic o", p=128), dma=True)
                pf_toks = [E("gpsimd", "dma_start", dst, src, dma=True)
                           for dst, src in prefetch]
                stf = [None, None, None]
                mm_free = [None, None]
                pend = None
                prev_ms = [None, None]
                prev_mq = [None, None]
                stats_free = None
                for it in range(NT + 1):
                    newpend = None
                    if it < NT:
                        par = it & 1
                        j = it % 3
                        b, t0 = it // 4, TT * (it % 4)
                        tk0 = 2048 * b + t0
                        ig = ing[par]
                        t_in = E("sync", "dma_start", ig[:],
                                 in_dram[:, :, tk0:tk0 + TT], dma=True,
                                 deps=[mm_free[par]])
                        if res_dram is x_fm:
                            t_res = E("sync", "dma_start", zt[j][:],
                                      x_fm[b].rearrange("(ic p) t -> p ic t", p=128)
                                      [:, :, t0:t0 + TT], dma=True,
                                      deps=[stf[j]])
                        else:
                            t_res = E("sync", "dma_start", zt[j][:],
                                      res_dram[:, :, tk0:tk0 + TT], dma=True,
                                      deps=[stf[j]])
                        t_z = []
                        for oc in range(8):
                            bk = oc & 1
                            mt = None
                            for ic in range(ICN):
                                mt = E("tensor", "matmul", pb[bk][:, 0:TT],
                                       w_bf[:, ic, 128 * oc:128 * oc + 128],
                                       ig[:, ic, :],
                                       start=(ic == 0), stop=(ic == ICN - 1),
                                       deps=[t_in, t_w, pb_free[bk]] if ic == 0 else [],
                                       inc=(ic == ICN - 1))
                            if scale_v is None:
                                ev = E("vector", "tensor_add", zt[j][:, oc, :],
                                       pb[bk][:, 0:TT], zt[j][:, oc, :],
                                       deps=[mt, t_res])
                            else:
                                ev = E("vector", "scalar_tensor_tensor",
                                       zt[j][:, oc, :],
                                       pb[bk][:, 0:TT], scale_v[:, oc:oc + 1],
                                       zt[j][:, oc, :], ALU.mult, ALU.add,
                                       deps=[mt, t_res])
                            pb_free[bk] = ev
                            t_z.append(ev)
                        mm_free[par] = mt
                        newpend = (it, j, t_z)
                    if pend is not None:
                        it0, j0, t_z = pend
                        b0, t00 = it0 // 4, TT * (it0 % 4)
                        tk00 = 2048 * b0 + t00
                        bkS, bkQ = (2, 3) if (it0 & 1) == 0 else (4, 5)
                        m_s = None
                        m_q = None
                        for oc in range(8):
                            t_cb = E("scalar", "activation", zbf[:, oc & 1, :],
                                     zt[j0][:, oc, :], AF.Copy,
                                     deps=[t_z[oc], prev_ms[oc & 1]])
                            t_cq = E("scalar", "activation", zsq[:, oc & 1, :],
                                     zt[j0][:, oc, :],
                                     AF.Square, deps=[t_z[oc], prev_mq[oc & 1]])
                            m_s = E("tensor", "matmul", pb[bkS][:, 0:TT],
                                    ones128[:], zbf[:, oc & 1, :],
                                    start=(oc == 0), stop=(oc == 7),
                                    deps=[t_cb, t_o128] + (
                                        [pb_free[bkS]] if oc == 0 else []),
                                    inc=True)
                            m_q = E("tensor", "matmul", pb[bkQ][:, 0:TT],
                                    ones128[:], zsq[:, oc & 1, :],
                                    start=(oc == 0), stop=(oc == 7),
                                    deps=[t_cq, t_o128] + (
                                        [pb_free[bkQ]] if oc == 0 else []),
                                    inc=True)
                            prev_ms[oc & 1] = m_s
                            prev_mq[oc & 1] = m_q
                        t_mu = E("vector", "tensor_scalar_mul", mu_t[:],
                                 pb[bkS][:, 0:TT], 1.0 / HID,
                                 deps=[m_s, stats_free])
                        pb_free[bkS] = t_mu
                        t_e2 = E("vector", "tensor_scalar_mul", e2_t[:],
                                 pb[bkQ][:, 0:TT], 1.0 / HID,
                                 deps=[m_q, stats_free])
                        pb_free[bkQ] = t_e2
                        t_m2 = E("vector", "tensor_tensor", m2_t[:],
                                 mu_t[:], mu_t[:], ALU.mult, deps=[t_mu])
                        t_var = E("vector", "tensor_sub", e2_t[:],
                                  e2_t[:], m2_t[:], deps=[t_e2, t_m2])
                        t_sd = E("scalar", "activation", m2_t[:], e2_t[:],
                                 AF.Ln, bias=epsv[:, 0:1],
                                 deps=[t_var, t_eps])
                        t_rs = E("scalar", "activation", e2_t[:], m2_t[:],
                                 AF.Exp, scale=-0.5,
                                 deps=[t_sd])
                        t_mr = E("vector", "tensor_tensor", mu_t[:],
                                 mu_t[:], e2_t[:], ALU.mult, deps=[t_rs])
                        t_o = []
                        for oc in range(8):
                            t_t1 = E("vector", "tensor_mul", zt[j0][:, oc, :],
                                     zt[j0][:, oc, :], e2_t[:],
                                     deps=[t_rs, t_z[oc]])
                            t_t2 = E("vector", "tensor_sub", zt[j0][:, oc, :],
                                     zt[j0][:, oc, :], mu_t[:],
                                     deps=[t_mr, t_t1])
                            t_o.append(t_t2)
                        stats_free = t_o[-1]
                        if out_is_ext:
                            st = E("gpsimd", "dma_start",
                                   out_fm[b0].rearrange(
                                       "(oc p) t -> p oc t", p=128)
                                   [:, :, t00:t00 + TT],
                                   zt[j0][:], dma=True, deps=t_o)
                        else:
                            st = E("gpsimd", "dma_start",
                                   out_f32_dram[:, :, tk00:tk00 + TT],
                                   zt[j0][:], dma=True, deps=t_o)
                        stf[j0] = st
                    pend = newpend
                return pf_toks

        s4ctx = ExitStack()
        if upto >= 4:
            w1b = s4ctx.enter_context(nc.sbuf_tensor("w1b", [128, 8, DH], BF16))
        if upto >= 3:
            pf = emit_ln_stage(
                "post", wpostT, attn_dram, True, x_fm, None,
                x1f_dram, False, "p",
                prefetch=([(w1b[:], w1T.rearrange("(ic p) o -> p ic o", p=128))]
                          if upto >= 4 else ()))
        else:
            pf = []

        sch.barrier()

        if upto >= 4:
            with s4ctx as sctx:
                w2b = sctx.enter_context(nc.sbuf_tensor("w2b", [128, 8, DH], BF16))
                x1g = [sctx.enter_context(nc.sbuf_tensor(f"x1g{i}", [128, 8, TT], BF16))
                       for i in range(2)]
                sil = sctx.enter_context(nc.sbuf_tensor("sil", [128, TT], F32))
                ast = sctx.enter_context(nc.sbuf_tensor("a_st", [128, 32, TT], BF16))
                t_w1 = pf[0] if pf else E(
                    "gpsimd", "dma_start", w1b[:],
                    w1T.rearrange("(ic p) o -> p ic o", p=128), dma=True)
                t_w2 = E("gpsimd", "dma_start", w2b[:],
                         w2T.rearrange("(ic p) o -> p ic o", p=128), dma=True)
                stf = [None, None]
                mm_free = [None, None]
                st_half = [None, None]
                for it in range(NT):
                    par = it & 1
                    b, t0 = it // 4, TT * (it % 4)
                    tk0 = 2048 * b + t0
                    xg = x1g[par]
                    t_in = E("gpsimd", "dma_start", xg[:],
                             x1f_dram[:, :, tk0:tk0 + TT], dma=True,
                             deps=[mm_free[par]])
                    t_as = []
                    prev_stt = None
                    for oc in range(32):
                        bk0, bk1 = (0, 1) if (oc & 1) == 0 else (2, 3)
                        m1 = None
                        for ic in range(8):
                            m1 = E("tensor", "matmul", pb[bk0][:, 0:TT],
                                   w1b[:, ic, 128 * oc:128 * oc + 128], xg[:, ic, :],
                                   start=(ic == 0), stop=(ic == 7),
                                   deps=[t_in, t_w1, pb_free[bk0]] if ic == 0 else [],
                                   inc=(ic == 7))
                        t_si = E("scalar", "activation", sil[:], pb[bk0][:, 0:TT],
                                 AF.Silu, scale=s1v[:, oc:oc + 1],
                                 deps=[m1, t_s1, prev_stt])
                        pb_free[bk0] = t_si
                        m2 = None
                        for ic in range(8):
                            m2 = E("tensor", "matmul", pb[bk1][:, 0:TT],
                                   w2b[:, ic, 128 * oc:128 * oc + 128], xg[:, ic, :],
                                   start=(ic == 0), stop=(ic == 7),
                                   deps=[t_in, t_w2, pb_free[bk1]] if ic == 0 else [],
                                   inc=(ic == 7))
                        ev = E("vector", "scalar_tensor_tensor", ast[:, oc, :],
                               pb[bk1][:, 0:TT], s2v[:, oc:oc + 1], sil[:],
                               ALU.mult, ALU.mult,
                               deps=[m2, t_si, t_s2, st_half[oc // 16]],
                               )
                        pb_free[bk1] = ev
                        prev_stt = ev
                        t_as.append(ev)
                        if oc == 15:
                            st_half[0] = E("scalar", "dma_start",
                                           a_dram[:, 0:16, tk0:tk0 + TT],
                                           ast[:, 0:16, :], dma=True,
                                           deps=t_as[0:16])
                    st_half[1] = E("scalar", "dma_start",
                                   a_dram[:, 16:32, tk0:tk0 + TT],
                                   ast[:, 16:32, :], dma=True, deps=t_as[16:32])
                    stf[par] = st_half[1]
                    mm_free[par] = m2

        sch.barrier()

        HPW = 256
        HNP = DH // HPW
        HST = 4
        HSUP = NT // HST

        def emit_h_stage(hT, hs_v, in_dram, out_dram, tag, bufs,
                         t_pn0=None, next_hT=None):
            panel, slab, bst = bufs
            hT_r = hT.rearrange("(ic p) o -> p ic o", p=128)
            stf = [None, None]
            pn_free = [None, None]
            t_pn = [None, None]
            t_pn[0] = t_pn0 if t_pn0 is not None else E(
                "gpsimd", "dma_start", panel[0][:], hT_r[:, :, 0:HPW],
                dma=True)
            t_sl = [None] * HST
            for j in range(HST):
                t_sl[j] = E("sync", "dma_start", slab[j][:],
                            in_dram[:, :, TT * j:TT * j + TT], dma=True)
            ret = None
            for sup in range(HSUP):
                for P in range(HNP):
                    pp = P & 1
                    last = (sup == HSUP - 1) and (P == HNP - 1)
                    if not last:
                        Pn = (P + 1) % HNP
                        t_pn[1 - pp] = E("gpsimd", "dma_start",
                                         panel[1 - pp][:],
                                         hT_r[:, :, HPW * Pn:HPW * Pn + HPW],
                                         dma=True, deps=[pn_free[1 - pp]])
                    elif next_hT is not None:
                        nhT_r = next_hT.rearrange("(ic p) o -> p ic o", p=128)
                        ret = E("gpsimd", "dma_start", panel[0][:],
                                nhT_r[:, :, 0:HPW], dma=True,
                                deps=[pn_free[0]])
                    last_mm = None
                    for j in range(HST):
                        it = HST * sup + j
                        tk0 = TT * it
                        evs = []
                        for oc in range(2):
                            bk = (2 * j + oc) & 7
                            mt = None
                            for ic in range(32):
                                mt = E("tensor", "matmul", pb[bk][:, 0:TT],
                                       panel[pp][:, ic, 128 * oc:128 * oc + 128],
                                       slab[j][:, ic, :],
                                       start=(ic == 0), stop=(ic == 31),
                                       deps=([t_sl[j], t_pn[pp], pb_free[bk]]
                                             if ic == 0 else []),
                                       inc=(ic == 31))
                            ev = E("scalar", "activation",
                                   bst[j & 1][:, oc, :],
                                   pb[bk][:, 0:TT], AF.Gelu,
                                   scale=hs_v[:, 2 * P + oc:2 * P + oc + 1],
                                   deps=[mt, stf[j & 1]])
                            pb_free[bk] = ev
                            evs.append(ev)
                            last_mm = mt
                        stf[j & 1] = E("scalar", "dma_start",
                                       out_dram[:, 2 * P:2 * P + 2,
                                                tk0:tk0 + TT],
                                       bst[j & 1][:], dma=True, deps=evs)
                        if P == HNP - 1 and sup + 1 < HSUP:
                            it2 = HST * (sup + 1) + j
                            t_sl[j] = E("sync", "dma_start", slab[j][:],
                                        in_dram[:, :, TT * it2:TT * it2 + TT],
                                        dma=True, deps=[last_mm])
                    pn_free[pp] = last_mm
            return ret

        if upto >= 5 or upto >= 6:
            with ExitStack() as hctx:
                hbufs = (
                    [hctx.enter_context(
                        nc.sbuf_tensor(f"pn_h{i}", [128, 32, HPW], BF16))
                     for i in range(2)],
                    [hctx.enter_context(
                        nc.sbuf_tensor(f"sl_h{j}", [128, 32, TT], BF16))
                     for j in range(HST)],
                    [hctx.enter_context(
                        nc.sbuf_tensor(f"bs_h{i}", [128, 2, TT], BF16))
                     for i in range(2)],
                )
                t_pn2 = None
                if upto >= 5:
                    t_pn2 = emit_h_stage(h1T, h1v, a_dram, b_dram, "h1", hbufs,
                                         next_hT=(h2T if upto >= 6 else None))
                sch.barrier()
                if upto >= 6:
                    emit_h_stage(h2T, h2v, b_dram, c_dram, "h2", hbufs,
                                 t_pn0=t_pn2)
        sch.barrier()

        if upto >= 7:
            emit_ln_stage("w3", w3T, c_dram, False, x1f_dram, w3v,
                          None, True, "y")

        sch.flush()
        sch.final_wait()

    return nc



def kernel(L0_attn_hidden_states, L0_attn_position_embeddings, L0_attn_w_q,
           L0_attn_w_k, L0_attn_w_v, L0_attn_w_rel, L0_attn_w_post,
           L0_attn_per_dim_scale, L0_ffn_w1_w, L0_ffn_w1_s, L0_ffn_w2_w,
           L0_ffn_w2_s, L0_ffn_h1_w, L0_ffn_h1_s, L0_ffn_h2_w, L0_ffn_h2_s,
           L0_ffn_w3_w, L0_ffn_w3_s, _upto=99, _dump=(), _nc=None):
    f32 = np.float32
    x = np.asarray(L0_attn_hidden_states, f32)
    B = x.shape[0]
    n_cores = 8
    bpc = B // n_cores

    def T(a):
        return np.ascontiguousarray(np.asarray(a, f32).T)

    common = {
        "wqT": T(L0_attn_w_q), "wkT": T(L0_attn_w_k), "wvT": T(L0_attn_w_v),
        "wrelT": T(L0_attn_w_rel), "wpostT": T(L0_attn_w_post),
        "posT": T(L0_attn_position_embeddings),
        "pds": np.ascontiguousarray(np.asarray(L0_attn_per_dim_scale, f32)),
        "w1T": T(L0_ffn_w1_w), "w2T": T(L0_ffn_w2_w),
        "h1T": T(L0_ffn_h1_w), "h2T": T(L0_ffn_h2_w), "w3T": T(L0_ffn_w3_w),
        "s1": np.ascontiguousarray(np.asarray(L0_ffn_w1_s, f32).reshape(32, 128).T),
        "s2": np.ascontiguousarray(np.asarray(L0_ffn_w2_s, f32).reshape(32, 128).T),
        "h1s": np.ascontiguousarray(np.asarray(L0_ffn_h1_s, f32).reshape(32, 128).T),
        "h2s": np.ascontiguousarray(np.asarray(L0_ffn_h2_s, f32).reshape(32, 128).T),
        "w3s": np.ascontiguousarray(np.asarray(L0_ffn_w3_s, f32).reshape(8, 128).T),
    }
    x_fm = np.ascontiguousarray(x.transpose(0, 2, 1))
    in_maps = []
    for c in range(n_cores):
        m = dict(common)
        m["x_fm"] = np.ascontiguousarray(x_fm[c * bpc:(c + 1) * bpc])
        in_maps.append(m)

    nc = _nc if _nc is not None else build_program(upto=_upto, dump=_dump)
    res = run_bass_kernel_spmd(nc, in_maps, core_ids=list(range(n_cores)))
    if _dump:
        return res
    out = np.concatenate([r["out_fm"] for r in res.results], axis=0)
    return np.ascontiguousarray(out.transpose(0, 2, 1))
